# version 08
# baseline (speedup 1.0000x reference)
import contextlib
import os
import numpy as np
import concourse.bass as bass
import concourse.mybir as mybir
from concourse.ap import AP
from concourse.bass_utils import run_bass_kernel_spmd

F32 = mybir.dt.float32
BF16 = mybir.dt.bfloat16
AF = mybir.ActivationFunctionType
ALU = mybir.AluOpType
AX = mybir.AxisListType

T = 4096
D = 1024
H = 16
NEX = 17
MC = 2048
NMC = T // MC
NTM = MC // 128
EPS = 1e-5
N_CORES = 8


class Sem:
    def __init__(self, nc, name):
        self.h = nc.alloc_semaphore(name)
        self.v = 0


class Bank:

    def __init__(self, t):
        self.t = t
        self.free = None

    def w(self):
        return [self.free] if self.free else []


class Prog:
    ENGINES = ("tensor", "vector", "scalar", "gpsimd", "sync")

    def __init__(self, nc):
        self.nc = nc
        self.ops = {e: [] for e in self.ENGINES}
        self.hw = {e: {} for e in self.ENGINES}

    def op(self, eng, fn, waits=(), inc=None, incn=1):
        ws = []
        for w in waits:
            if w is None:
                continue
            if isinstance(w, tuple):
                s, v = w
            else:
                s, v = w, w.v
            if v <= 0:
                continue
            ws.append((s.h, v))
        hwe = self.hw[eng]
        ws2 = []
        for (h, v) in ws:
            key = h.num
            if hwe.get(key, -1) < v:
                hwe[key] = v
                ws2.append((h, v))
        if inc is not None:
            inc.v += incn

        def run(e, fn=fn, ws2=ws2, inc=inc, incn=incn):
            for (h, v) in ws2:
                e.wait_ge(h, v)
            inst = fn(e)
            if inc is not None:
                inst.then_inc(inc.h, incn)

        self.ops[eng].append(run)

    def flush(self):
        with self.nc.Block() as blk:
            for name in self.ENGINES:
                if self.ops[name]:
                    getattr(blk, name)(
                        lambda eng, _n=name: [f(eng) for f in self.ops[_n]])
        self.ops = {e: [] for e in self.ENGINES}
        self.hw = {e: {} for e in self.ENGINES}


def build_program():
    nc = bass.Bass("TRN2", target_bir_lowering=False)

    x_in = nc.dram_tensor("x_in", [T, D], F32, kind="ExternalInput").ap()
    w13t = nc.dram_tensor("w13t", [NEX, D, D], F32, kind="ExternalInput").ap()
    w2t = nc.dram_tensor("w2t", [NEX, 512, D], F32, kind="ExternalInput").ap()
    gwt = nc.dram_tensor("gwt", [D, 16], F32, kind="ExternalInput").ap()
    gb_b = nc.dram_tensor("gb_b", [128, 256], F32, kind="ExternalInput").ap()
    wq = nc.dram_tensor("wq", [D, D], F32, kind="ExternalInput").ap()
    wkv = nc.dram_tensor("wkv", [D, 2 * D], F32, kind="ExternalInput").ap()
    wo = nc.dram_tensor("wo", [D, D], F32, kind="ExternalInput").ap()
    bo_b = nc.dram_tensor("bo_b", [128, D], F32, kind="ExternalInput").ap()
    out = nc.dram_tensor("out", [T, D], F32, kind="ExternalOutput").ap()
    _tap = "ExternalOutput" if os.environ.get("KMK_DEBUG_TAPS") else "Internal"
    x1 = nc.dram_tensor("x1s", [T, D], F32, kind=_tap).ap()
    x2 = nc.dram_tensor("x2s", [T, D], F32, kind=_tap).ap()

    with nc.cleanup_on_exit():
        ident_bf = nc.alloc_sbuf_tensor("ident_bf", [128, 128], BF16)
        ident_f = nc.alloc_sbuf_tensor("ident_f", [128, 128], F32)
        consts = nc.alloc_sbuf_tensor("consts", [128, 4], F32)
        gbb_sb = nc.alloc_sbuf_tensor("gbb_sb", [128, 256], F32)
        bob_sb = nc.alloc_sbuf_tensor("bob_sb", [128, D], F32)
        gwt_sb = nc.alloc_sbuf_tensor("gwt_sb", [128, 8 * 16], BF16)
        B = [None] * 8

        mm = Sem(nc, "mm"); ev = Sem(nc, "ev"); es = Sem(nc, "es")
        gv = Sem(nc, "gv"); ldi = Sem(nc, "ldi")
        ldw2 = [Sem(nc, "ldwa"), Sem(nc, "ldwb")]
        lx2 = [Sem(nc, "lxa"), Sem(nc, "lxb")]
        st2 = [Sem(nc, "sta"), Sem(nc, "stb")]

        P = Prog(nc)

        P.op("gpsimd", lambda g: g.memset(ident_bf[:], 0.0), waits=[gv], inc=gv)
        P.op("gpsimd", lambda g: g.affine_select(
            out=ident_bf[:], in_=ident_bf[:], compare_op=ALU.not_equal,
            fill=1.0, base=0, pattern=[[-1, 128]], channel_multiplier=1),
            waits=[gv], inc=gv)
        P.op("gpsimd", lambda g: g.memset(ident_f[:], 0.0), waits=[gv], inc=gv)
        P.op("gpsimd", lambda g: g.affine_select(
            out=ident_f[:], in_=ident_f[:], compare_op=ALU.not_equal,
            fill=1.0, base=0, pattern=[[-1, 128]], channel_multiplier=1),
            waits=[gv], inc=gv)
        P.op("gpsimd", lambda g: g.memset(consts[:, 0:1], EPS), waits=[gv], inc=gv)
        P.op("gpsimd", lambda g: g.memset(consts[:, 1:2], -0.5), waits=[gv], inc=gv)
        P.op("gpsimd", lambda g: g.memset(consts[:, 2:3], 0.125), waits=[gv], inc=gv)
        P.op("gpsimd", lambda g: g.dma_start(gbb_sb[:], gb_b), inc=ldi, incn=16)
        P.op("gpsimd", lambda g: g.dma_start(bob_sb[:], bo_b), inc=ldi, incn=16)
        P.op("gpsimd", lambda g: g.dma_start(
            gwt_sb[:], gwt.rearrange("(kc p) e -> p kc e", p=128)), inc=ldi, incn=16)
        init_snap = (ldi, ldi.v)
        gv_snap = (gv, gv.v)

        def quiesce():
            P.op("gpsimd", lambda g: g.nop(),
                 waits=[ldw2[0], ldw2[1], lx2[0], lx2[1], st2[0], st2[1], ldi])

        def emit_ln(src, row0, ntiles, xsl, xsl_free, lxs, ln_tm, ln_tm_free,
                    stt, mv, rstd, ln_fm_sb, fm_cols, tp_banks, dst_war=None,
                    keep_x=None):
            for ti in range(ntiles):
                slot = xsl[ti % len(xsl)]
                lsem = lxs[ti % len(xsl)]
                P.op("sync", lambda sy, s=slot, r=row0 + ti * 128:
                     sy.dma_start(s[:], src[r:r + 128, :]),
                     waits=[xsl_free[ti % len(xsl)]], inc=lsem, incn=16)
                ld_snap = (lsem, lsem.v)
                P.op("vector", lambda v, s=slot: v.bn_stats(stt[:, 0:6], s[:, 0:512]),
                     waits=[ld_snap, ev])
                P.op("vector", lambda v, s=slot: v.bn_stats(stt[:, 6:12], s[:, 512:1024]),
                     inc=ev)
                P.op("vector", lambda v, ti=ti: v.bn_aggr(mv[:, 2 * ti:2 * ti + 2], stt[:]),
                     waits=[ev], inc=ev)
                P.op("scalar", lambda s, ti=ti: s.activation(
                    rstd[:, ti:ti + 1], mv[:, 2 * ti + 1:2 * ti + 2], AF.Ln,
                    bias=consts[:, 0:1]), waits=[ev, es, gv_snap], inc=es)
                P.op("scalar", lambda s, ti=ti: s.activation(
                    rstd[:, ti:ti + 1], rstd[:, ti:ti + 1], AF.Exp,
                    scale=consts[:, 1:2]), waits=[es], inc=es)
                tslot = ln_tm[ti % len(ln_tm)]
                P.op("vector", lambda v, s=slot, t=tslot, ti=ti: v.tensor_scalar(
                    t[:], s[:], mv[:, 2 * ti:2 * ti + 1], rstd[:, ti:ti + 1],
                    ALU.subtract, ALU.mult),
                    waits=[es, ev, ln_tm_free[ti % len(ln_tm)]], inc=ev)
                apply_snap = (ev, ev.v)
                if keep_x is not None:
                    keep_x[ti] = slot
                else:
                    xsl_free[ti % len(xsl)] = apply_snap
                for bk in range(2):
                    bank = tp_banks[bk]
                    P.op("tensor", lambda t, tt=tslot, b=bank.t, bk=bk: [
                        t.matmul(b[:, j * 128:(j + 1) * 128],
                                 tt[:, (bk * 4 + j) * 128:(bk * 4 + j + 1) * 128],
                                 ident_bf[:], start=True, stop=True)
                        for j in range(4)][-1],
                        waits=[apply_snap, gv_snap] + bank.w(), inc=mm)
                    mmsnap = (mm, mm.v)
                    P.op("vector", lambda v, b=bank.t, ti=ti, bk=bk: v.tensor_copy(
                        AP(tensor=ln_fm_sb.ap().tensor,
                           offset=(bk * 4) * fm_cols + ti * 128,
                           ap=[ln_fm_sb.ap().ap[0], [fm_cols, 4], [1, 128]]),
                        b[:, 0:512]),
                        waits=[mmsnap, dst_war], inc=ev)
                    bank.free = (ev, ev.v)
                ln_tm_free[ti % len(ln_tm)] = (mm, mm.v)
            return (ev, ev.v)

        def emit_ffn_pass(fb, src, dst):
            ln_fm = fb["ln_fm"]; y_acc = fb["y_acc"]
            w13b = fb["w13b"]; w2b = fb["w2b"]
            xsl = fb["xsl"]; ln_tm = fb["ln_tm"]
            h_sb = fb["h_sb"]; sg = fb["sg"]
            sc_tm = fb["sc_tm"]; bt = fb["bt"]; msk = fb["msk"]; gsum = fb["gsum"]
            stt = fb["stt"]; mv = fb["mv"]; rstd = fb["rstd"]

            w13_free = [None, None]; w2_free = [None, None]
            xsl_free = fb["xsl_free"]
            ln_tm_free = fb["ln_tm_free"]
            sg_free = [None, None]
            h_war = {}
            yacc_free = None
            lnfm_war = None

            def preload(idx):
                e = idx % NEX
                buf = idx % 2
                lsem = ldw2[buf]
                P.op("gpsimd", lambda g, e=e, b=buf: g.dma_start(
                    w13b[b][:], w13t[e].rearrange("(kc p) j -> p kc j", p=128)),
                    waits=[w13_free[buf]], inc=lsem, incn=16)
                P.op("gpsimd", lambda g, e=e, b=buf: g.dma_start(
                    w2b[b][:], w2t[e].rearrange("(ic p) d -> p ic d", p=128)),
                    waits=[w2_free[buf]], inc=lsem, incn=16)
                return (lsem, lsem.v)

            w_snaps = {0: preload(0)}
            sidx = 0
            gu_banks = [B[0], B[1], B[2]]
            y_banks = [B[3], B[4], B[5], B[6], B[7]]
            gu_i = [0]; y_i = [0]

            for mc in range(NMC):
                lnfm_ready = emit_ln(src, mc * MC, NTM, xsl, xsl_free, lx2, ln_tm,
                                     ln_tm_free, stt, mv, rstd, ln_fm, MC,
                                     [B[0], B[1]], dst_war=lnfm_war)
                gate_bank = B[2]
                for half in range(2):
                    P.op("tensor", lambda t, half=half, b=gate_bank.t: [
                        t.matmul(b[:, tj * 16:tj * 16 + 16],
                                 ln_fm[:, kc * MC + (half * 8 + tj) * 128:
                                       kc * MC + (half * 8 + tj) * 128 + 128],
                                 gwt_sb[:, kc * 16:kc * 16 + 16],
                                 start=(kc == 0), stop=(kc == 7))
                        for tj in range(8) for kc in range(8)][-1],
                        waits=[lnfm_ready, init_snap] + gate_bank.w(), inc=mm)
                    mmsnap = (mm, mm.v)
                    P.op("scalar", lambda s, half=half, b=gate_bank.t: s.activation(
                        sc_tm[:, half * 128:half * 128 + 128], b[:, 0:128],
                        AF.Sigmoid), waits=[mmsnap], inc=es)
                    gate_bank.free = (es, es.v)

                GE = 16
                P.op("vector", lambda v: v.tensor_add(bt[0][:], sc_tm[:], gbb_sb[:]),
                     waits=[es, ev], inc=ev)

                def lanes_ap(tens, fixed):
                    base = tens.ap()
                    off = sum(b_ for b_, val in fixed.items() if val)
                    dims = [[GE, 16]]
                    for b_ in (8, 4, 2, 1):
                        if b_ not in fixed:
                            dims.append([b_, 2])
                    merged = [dims[0]]
                    for d_ in dims[1:]:
                        o = merged[-1]
                        if o[0] == d_[0] * d_[1]:
                            merged[-1] = [d_[0], d_[1] * o[1]]
                        else:
                            merged.append(d_)
                    return AP(tensor=base.tensor, offset=off,
                              ap=[base.ap[0]] + merged)

                srcb, dstb = bt[0], bt[1]
                for k in (2, 4, 8, 16):
                    j = k // 2
                    while j >= 1:
                        pairs = [({j: 0}, {j: 1}, ALU.max, ALU.min)] if k == 16 else [
                            ({j: 0, k: 0}, {j: 1, k: 0}, ALU.max, ALU.min),
                            ({j: 0, k: 1}, {j: 1, k: 1}, ALU.min, ALU.max)]
                        first = True
                        for (lo, hi, o1, o2) in pairs:
                            P.op("vector", lambda v, s=srcb, d2=dstb, lo=lo, hi=hi, o1=o1:
                                 v.tensor_tensor(lanes_ap(d2, lo), lanes_ap(s, lo),
                                                 lanes_ap(s, hi), o1),
                                 waits=([ev] if first else []))
                            first = False
                            P.op("vector", lambda v, s=srcb, d2=dstb, lo=lo, hi=hi, o2=o2:
                                 v.tensor_tensor(lanes_ap(d2, hi), lanes_ap(s, lo),
                                                 lanes_ap(s, hi), o2),
                                 inc=ev)
                        srcb, dstb = dstb, srcb
                        j //= 2
                P.op("vector", lambda v, d2=dstb: v.tensor_add(d2[:], sc_tm[:], gbb_sb[:]),
                     waits=[ev], inc=ev)
                theta = AP(tensor=srcb.ap().tensor, offset=3,
                           ap=[srcb.ap().ap[0], [GE, 16], [0, GE]])

                def g3(t_):
                    return AP(tensor=t_.ap().tensor, offset=0,
                              ap=[t_.ap().ap[0], [GE, 16], [1, GE]])

                P.op("vector", lambda v, d2=dstb: v.tensor_tensor(
                    g3(msk), g3(d2), theta, ALU.is_ge), waits=[ev], inc=ev)
                P.op("vector", lambda v: v.tensor_mul(msk[:], sc_tm[:], msk[:]),
                     waits=[ev], inc=ev)
                P.op("vector", lambda v: v.reduce_sum(gsum[:], g3(msk), AX.X),
                     waits=[ev], inc=ev)
                P.op("vector", lambda v: v.reciprocal(gsum[:], gsum[:]), waits=[ev], inc=ev)
                P.op("vector", lambda v: v.tensor_scalar(
                    gsum[:], gsum[:], 2.5, None, ALU.mult), waits=[ev], inc=ev)
                rb = AP(tensor=gsum.ap().tensor, offset=0,
                        ap=[gsum.ap().ap[0], [1, 16], [0, GE]])
                P.op("vector", lambda v: v.tensor_tensor(g3(msk), g3(msk), rb, ALU.mult),
                     waits=[ev], inc=ev)
                msk_ready = (ev, ev.v)

                for e in range(NEX):
                    buf = sidx % 2
                    w_snap = w_snaps[sidx]
                    w_snaps[sidx + 1] = preload(sidx + 1)
                    is_shared = (e == NEX - 1)

                    h_readys = {}

                    def emit_y(s):
                            for tsub in range(4):
                                cc = (s * 4 + tsub) * 16 + e
                                for dhf in range(2):
                                    yi = y_i[0]; y_i[0] += 1
                                    bank = y_banks[yi % 5]
                                    P.op("tensor", lambda t, s=s, tsub=tsub, dhf=dhf,
                                         b=buf, bk=bank.t: [
                                             t.matmul(bk[:, 0:512],
                                                      h_sb[:, ic * MC + s * 512 + tsub * 128:
                                                            ic * MC + s * 512 + tsub * 128 + 128],
                                                      w2b[b][:, ic * 1024 + dhf * 512:
                                                            ic * 1024 + dhf * 512 + 512],
                                                      start=(ic == 0), stop=(ic == 3))
                                             for ic in range(4)][-1],
                                        waits=[h_readys[s], w_snap] + bank.w(), inc=mm)
                                    mmsnap = (mm, mm.v)
                                    ycol = (s * 4 + tsub) * 1024 + dhf * 512
                                    if is_shared:
                                        P.op("vector", lambda v, ycol=ycol, bk=bank.t:
                                             v.tensor_tensor(y_acc[:, ycol:ycol + 512],
                                                             bk[:, 0:512],
                                                             y_acc[:, ycol:ycol + 512],
                                                             ALU.add),
                                             waits=[mmsnap, ev], inc=ev)
                                    elif e == 0:
                                        P.op("vector", lambda v, ycol=ycol, bk=bank.t,
                                             cc=cc: v.tensor_scalar(
                                                 y_acc[:, ycol:ycol + 512], bk[:, 0:512],
                                                 msk[:, cc:cc + 1], None, ALU.mult),
                                             waits=[mmsnap, msk_ready, yacc_free], inc=ev)
                                    else:
                                        P.op("vector", lambda v, ycol=ycol, bk=bank.t,
                                             cc=cc: v.scalar_tensor_tensor(
                                                 y_acc[:, ycol:ycol + 512], bk[:, 0:512],
                                                 msk[:, cc:cc + 1],
                                                 y_acc[:, ycol:ycol + 512],
                                                 ALU.mult, ALU.add),
                                             waits=[mmsnap, ev], inc=ev)
                                    bank.free = (ev, ev.v)
                            h_war[s] = (mm, mm.v)

                    for s in range(4):
                        sg_snap = None
                        for pair in range(4):
                            sslot = pair % 2
                            for gate_part in (True, False):
                                jg = pair if gate_part else pair + 4
                                gi = gu_i[0]; gu_i[0] += 1
                                bank = gu_banks[gi % 3]
                                P.op("tensor", lambda t, jg=jg, s=s, b=buf, bk=bank.t: [
                                    t.matmul(bk[:, 0:512],
                                             w13b[b][:, kc * 1024 + jg * 128:
                                                   kc * 1024 + jg * 128 + 128],
                                             ln_fm[:, kc * MC + s * 512:
                                                   kc * MC + s * 512 + 512],
                                             start=(kc == 0), stop=(kc == 7))
                                    for kc in range(8)][-1],
                                    waits=[w_snap, lnfm_ready] + bank.w(), inc=mm)
                                mmsnap = (mm, mm.v)
                                if gate_part:
                                    P.op("scalar", lambda sc, sslot=sslot, bk=bank.t:
                                         sc.activation(sg[sslot][:], bk[:, 0:512],
                                                       AF.Silu),
                                         waits=[mmsnap, sg_free[sslot]], inc=es)
                                    sg_snap = (es, es.v)
                                    bank.free = (es, es.v)
                                else:
                                    ic = pair
                                    hcol = ic * MC + s * 512
                                    hw_ = h_war.get(s)
                                    P.op("vector", lambda v, hcol=hcol, bk=bank.t,
                                         sslot=sslot: v.tensor_tensor(
                                             h_sb[:, hcol:hcol + 512],
                                             bk[:, 0:512], sg[sslot][:],
                                             ALU.mult),
                                         waits=[mmsnap, sg_snap, hw_], inc=ev)
                                    bank.free = (ev, ev.v)
                                    sg_free[sslot] = (ev, ev.v)
                        h_readys[s] = (ev, ev.v)
                        if s >= 1:
                            emit_y(s - 1)
                    emit_y(3)
                    w13_free[buf] = (mm, mm.v)
                    w2_free[buf] = (mm, mm.v)
                    sidx += 1
                lnfm_war = (mm, mm.v)
                y_done = (ev, ev.v)

                for ti in range(NTM):
                    slot = xsl[ti % 2]
                    lsem = lx2[ti % 2]
                    ssem = st2[ti % 2]
                    P.op("sync", lambda sy, s=slot, r=mc * MC + ti * 128:
                         sy.dma_start(s[:], src[r:r + 128, :]),
                         waits=[xsl_free[ti % 2]], inc=lsem, incn=16)
                    ld_snap = (lsem, lsem.v)
                    P.op("vector", lambda v, s=slot, ti=ti:
                         v.scalar_tensor_tensor(
                             s[:], y_acc[:, ti * 1024:ti * 1024 + 1024], 0.5, s[:],
                             ALU.mult, ALU.add),
                         waits=[ld_snap, y_done], inc=ev)
                    evsnap = (ev, ev.v)
                    P.op("sync", lambda sy, s=slot, r=mc * MC + ti * 128:
                         sy.dma_start(dst[r:r + 128, :], s[:]),
                         waits=[evsnap], inc=ssem, incn=16)
                    xsl_free[ti % 2] = (ssem, ssem.v)
                yacc_free = (ev, ev.v)

        def emit_attn_b1(ab):
            K_fm = ab["K_fm"]; v_tm = ab["v_tm"]; wkv_sb = ab["wkv_sb"]
            ln_fm2 = ab["ln_fm"]; ln_tm = ab["ln_tm"]; xsl = ab["xsl"]
            stt = ab["stt"]; mv = ab["mv"]; rstd = ab["rstd"]
            xsl_free = ab["xsl_free"]; ln_tm_free = ab["ln_tm_free"]
            lnfm_war = [None, None]

            P.op("gpsimd", lambda g: g.dma_start(
                wkv_sb[:], wkv.rearrange("(kc p) n -> p kc n", p=128)),
                inc=ldi, incn=16)
            wkv_snap = (ldi, ldi.v)
            P.op("vector", lambda v: v.memset(v_tm[:], 1.0), waits=[ev], inc=ev)
            vmemset = (ev, ev.v)

            kb = [B[2], B[3]]
            vb = [B[4], B[5]]
            for c in range(8):
                lf = c % 2
                ln_ready = emit_ln(x1, c * 512, 4, xsl, xsl_free, lx2, ln_tm,
                                   ln_tm_free, stt, mv, rstd, ln_fm2[lf], 512,
                                   [B[0], B[1]], dst_war=lnfm_war[lf])
                for ic in range(8):
                    bank = kb[ic % 2]
                    P.op("tensor", lambda t, ic=ic, lf=lf, bk=bank.t: [
                        t.matmul(bk[:, 0:512],
                                 wkv_sb[:, kc * 2048 + ic * 128:kc * 2048 + ic * 128 + 128],
                                 ln_fm2[lf][:, kc * 512:kc * 512 + 512],
                                 start=(kc == 0), stop=(kc == 7))
                        for kc in range(8)][-1],
                        waits=[ln_ready, wkv_snap] + bank.w(), inc=mm)
                    mmsnap = (mm, mm.v)
                    P.op("scalar", lambda s, ic=ic, c=c, bk=bank.t: s.copy(
                        K_fm[:, ic * T + c * 512:ic * T + c * 512 + 512],
                        bk[:, 0:512]), waits=[mmsnap], inc=es)
                    bank.free = (es, es.v)
                for tsub in range(4):
                    for vh in range(2):
                        gi = tsub * 2 + vh
                        bank = vb[gi % 2]
                        P.op("tensor", lambda t, tsub=tsub, vh=vh, lf=lf, bk=bank.t: [
                            t.matmul(bk[:, 0:512],
                                     ln_fm2[lf][:, kc * 512 + tsub * 128:
                                                kc * 512 + tsub * 128 + 128],
                                     wkv_sb[:, kc * 2048 + 1024 + vh * 512:
                                            kc * 2048 + 1024 + vh * 512 + 512],
                                     start=(kc == 0), stop=(kc == 7))
                            for kc in range(8)][-1],
                            waits=[ln_ready, wkv_snap] + bank.w(), inc=mm)
                        mmsnap = (mm, mm.v)
                        row = (c * 4 + tsub) * 1040 + vh * 8 * 65
                        P.op("vector", lambda v, row=row, bk=bank.t: v.tensor_copy(
                            AP(tensor=v_tm.ap().tensor, offset=row,
                               ap=[v_tm.ap().ap[0], [65, 8], [1, 64]]),
                            bk[:, 0:512]), waits=[mmsnap, vmemset], inc=ev)
                        bank.free = (ev, ev.v)
                lnfm_war[lf] = (mm, mm.v)

        def emit_attn_b2(ab):
            K_fm = ab["K_fm"]; v_tm = ab["v_tm"]
            wq_sb = ab["wq_sb"]; wo_sb = ab["wo_sb"]
            E2 = ab["E2"]; q_sb = ab["q_sb"]; at_sb = ab["at_sb"]; cc_sb = ab["cc_sb"]
            xsl = ab["xsl"]; ln_tm = ab["ln_tm"]; den = ab["den"]
            stt = ab["stt"]; mv = ab["mv"]; rstd = ab["rstd"]
            ln_fmq = ab["ln_fmq"]
            xsl_free = ab["xsl_free"]; ln_tm_free = ab["ln_tm_free"]

            P.op("gpsimd", lambda g: g.dma_start(
                wq_sb[:], wq.rearrange("(kc p) n -> p kc n", p=128)), inc=ldi, incn=16)
            P.op("gpsimd", lambda g: g.dma_start(
                wo_sb[:], wo.rearrange("(kc p) n -> p kc n", p=128)), inc=ldi, incn=16)
            w_snap = (ldi, ldi.v)

            E_free = [None, None, None, None]
            proj = [B[2], B[3]]
            scb = [B[4], B[5]]
            scb2 = [B[6], B[7]]
            pvb = B[0]
            pvb2 = B[1]
            atb = [at_sb, ln_tm[1]]
            st_ctx = {"at_war": [None, None], "cc_war": None, "lnfmq_war": None,
                      "at": at_sb, "atq": 0}

            def emit_pv_half(h, pvbk, exp_sn, ebufs, half):
                eb = E2[ebufs[half]]
                for gg in range(4):
                    g = half * 4 + gg
                    P.op("tensor", lambda t, g=g, gg=gg, h=h, eb=eb, bk=pvbk.t: [
                        t.matmul(bk[:, 0:65],
                                 eb[:, (gg * 4 + cj) * 128:(gg * 4 + cj) * 128 + 128],
                                 v_tm[:, (g * 4 + cj) * 1040 + h * 65:
                                       (g * 4 + cj) * 1040 + h * 65 + 65],
                                 start=(g == 0 and cj == 0),
                                 stop=(g == 7 and cj == 3))
                        for cj in range(4)][-1],
                        waits=[exp_sn[g]] + (pvbk.w() if g == 0 else []),
                        inc=mm)
                E_free[ebufs[half]] = (mm, mm.v)
                if half == 1:
                    pv_done = (mm, mm.v)
                    P.op("vector", lambda v, bk=pvbk.t: v.reciprocal(
                        den[:, 0:1], bk[:, 64:65]), waits=[pv_done, ev], inc=ev)
                    P.op("vector", lambda v, h=h, a=st_ctx["at"], bk=pvbk.t:
                         v.tensor_scalar(
                             a[:, h * 64:h * 64 + 64], bk[:, 0:64], den[:, 0:1],
                             None, ALU.mult),
                         waits=[ev] + ([st_ctx["at_war"][st_ctx["atq"] % 2]]
                                       if h == 0 else []), inc=ev)
                    pvbk.free = (ev, ev.v)

            def emit_qk_g(g, ic, exp_snaps, h0, h1, q_ready):
                bank0 = scb[g % 2]
                bank1 = scb2[g % 2]
                P.op("tensor", lambda t, g=g, ic=ic, b0=bank0.t, b1=bank1.t: [
                    t.matmul((b0 if hh == 0 else b1)[:, cj * 128:cj * 128 + 128],
                             K_fm.ap()[hh * 64:hh * 64 + 64,
                                       ic * T + (g * 4 + cj) * 128:
                                       ic * T + (g * 4 + cj) * 128 + 128],
                             q_sb.ap()[hh * 64:hh * 64 + 64,
                                       ic * 128:ic * 128 + 128],
                             start=True, stop=True, tile_position=(hh * 64, 0))
                    for cj in range(4) for hh in (0, 1)][-1],
                    waits=[q_ready] + bank0.w() + bank1.w(), inc=mm)
                mmsnap = (mm, mm.v)
                ecol = (g % 4) * 512
                for (h, bank, ebi) in ((h0, bank0, 0 if g < 4 else 1),
                                       (h1, bank1, 2 if g < 4 else 3)):
                    P.op("scalar", lambda s, ecol=ecol, eb=E2[ebi], bk=bank.t:
                         s.activation(eb[:, ecol:ecol + 512], bk[:, 0:512], AF.Exp),
                         waits=[mmsnap, E_free[ebi]], inc=es)
                    exp_snaps[h].append((es, es.v))
                    bank.free = (es, es.v)

            def prep_q(q):
                keep = {}
                sub_free = [xsl_free[q % 2]]
                ln_ready = emit_ln(x1, q * 128, 1, [xsl[q % 2]], sub_free,
                                   [lx2[q % 2]], ln_tm, ln_tm_free,
                                   stt, mv, rstd, ln_fmq, 128, [proj[0], proj[1]],
                                   dst_war=st_ctx["lnfmq_war"], keep_x=keep)
                for bk2 in range(2):
                    bank = proj[bk2]
                    P.op("tensor", lambda t, bk2=bk2, bk=bank.t: [
                        t.matmul(bk[:, gj * 128:gj * 128 + 128],
                                 wq_sb[:, kc * 1024 + (bk2 * 4 + gj) * 128:
                                       kc * 1024 + (bk2 * 4 + gj) * 128 + 128],
                                 ln_fmq[:, kc * 128:kc * 128 + 128],
                                 start=(kc == 0), stop=(kc == 7))
                        for gj in range(4) for kc in range(8)][-1],
                        waits=[ln_ready, w_snap] + bank.w(), inc=mm)
                st_ctx["lnfmq_war"] = (mm, mm.v)
                return {"xq": keep[0], "qmm": (mm, mm.v)}

            def emit_qevac(qs):
                for bk2 in range(2):
                    bank = proj[bk2]
                    P.op("scalar", lambda sc, bk2=bk2, bk=bank.t: sc.activation(
                        q_sb[:, bk2 * 512:bk2 * 512 + 512], bk[:, 0:512], AF.Copy,
                        scale=consts[:, 2:3]), waits=[qs["qmm"], gv_snap, es], inc=es)
                    bank.free = (es, es.v)
                return (es, es.v)

            def emit_tail_a(q):
                a = atb[q % 2]
                at_ready = (ev, ev.v)
                for rnd in range(4):
                    bank = (scb + scb2)[rnd]
                    P.op("tensor", lambda t, rnd=rnd, a=a, bk=bank.t: [
                        t.matmul(bk.ap()[0:64, j * 128:j * 128 + 128],
                                 a[:, (rnd * 4 + j) * 64:(rnd * 4 + j + 1) * 64],
                                 ident_bf[:], start=True, stop=True)
                        for j in range(4)][-1],
                        waits=[at_ready, gv_snap] + bank.w(), inc=mm)
                    mmsnap = (mm, mm.v)
                    for j in range(4):
                        hh = rnd * 4 + j
                        P.op("vector", lambda v, hh=hh, j=j, bk=bank.t: v.tensor_copy(
                            cc_sb.ap()[(hh % 2) * 64:(hh % 2) * 64 + 64,
                                       (hh // 2) * 128:(hh // 2) * 128 + 128],
                            bk.ap()[0:64, j * 128:j * 128 + 128]),
                            waits=[mmsnap] + ([st_ctx["cc_war"]] if (rnd == 0 and j == 0) else []),
                            inc=ev)
                    bank.free = (ev, ev.v)
                st_ctx["at_war"][q % 2] = (mm, mm.v)
                return (ev, ev.v)

            def emit_tail_b(q, qs, cc_ready):
                xq = qs["xq"]
                P.op("vector", lambda v, xq=xq: v.tensor_tensor(
                    xq[:], xq[:], bob_sb[:], ALU.add), waits=[ev], inc=ev)
                for dhf in range(2):
                    bank = proj[dhf]
                    P.op("tensor", lambda t, dhf=dhf, bk=bank.t: [
                        t.matmul(bk[:, 0:512],
                                 cc_sb[:, ic2 * 128:ic2 * 128 + 128],
                                 wo_sb[:, ic2 * 1024 + dhf * 512:
                                       ic2 * 1024 + dhf * 512 + 512],
                                 start=(ic2 == 0), stop=(ic2 == 7))
                        for ic2 in range(8)][-1],
                        waits=[cc_ready, w_snap] + bank.w(), inc=mm)
                    mmsnap = (mm, mm.v)
                    P.op("vector", lambda v, dhf=dhf, xq=xq, bk=bank.t:
                         v.tensor_tensor(xq[:, dhf * 512:dhf * 512 + 512],
                                         bk[:, 0:512],
                                         xq[:, dhf * 512:dhf * 512 + 512],
                                         ALU.add),
                         waits=[mmsnap, ev], inc=ev)
                    bank.free = (ev, ev.v)
                st_ctx["cc_war"] = (mm, mm.v)
                evsnap = (ev, ev.v)
                ssem = st2[q % 2]
                P.op("sync", lambda sy, xq=xq, q=q, ssem=ssem: sy.dma_start(
                    x2[q * 128:(q + 1) * 128, :], xq[:]), waits=[evsnap],
                    inc=ssem, incn=16)
                xsl_free[q % 2] = (ssem, ssem.v)

            qs_cur = prep_q(0)
            q_ready = emit_qevac(qs_cur)
            qs_next = None
            tail_q = None
            tail_qs = None
            cc_ready = None
            for q in range(32):
                st_ctx["at"] = atb[q % 2]
                st_ctx["atq"] = q
                pend = {}
                for hp in range(H // 2):
                    h0, h1 = 2 * hp, 2 * hp + 1
                    ic = hp
                    exp_snaps = {h0: [], h1: []}
                    if "h1A" in pend:
                        emit_pv_half(*pend.pop("h1A"), 0)
                    for g in range(4):
                        emit_qk_g(g, ic, exp_snaps, h0, h1, q_ready)
                    if "h0B" in pend:
                        emit_pv_half(*pend.pop("h0B"), 1)
                    if "h1B" in pend:
                        emit_pv_half(*pend.pop("h1B"), 1)
                    for g in range(4, 8):
                        emit_qk_g(g, ic, exp_snaps, h0, h1, q_ready)
                    if hp == 2 and tail_q is not None:
                        cc_ready = emit_tail_a(tail_q)
                    if hp == 3 and tail_q is not None:
                        emit_tail_b(tail_q, tail_qs, cc_ready)
                        tail_q = None
                    if hp == 4 and q + 1 < 32:
                        qs_next = prep_q(q + 1)
                    emit_pv_half(h0, pvb, exp_snaps[h0], (0, 1), 0)
                    pend["h0B"] = (h0, pvb, exp_snaps[h0], (0, 1))
                    pend["h1A"] = (h1, pvb2, exp_snaps[h1], (2, 3))
                    pend["h1B"] = (h1, pvb2, exp_snaps[h1], (2, 3))
                emit_pv_half(*pend.pop("h1A"), 0)
                emit_pv_half(*pend.pop("h0B"), 1)
                emit_pv_half(*pend.pop("h1B"), 1)
                tail_q = q
                tail_qs = qs_cur
                if q + 1 < 32:
                    q_ready = emit_qevac(qs_next)
                    qs_cur = qs_next
            cc_ready = emit_tail_a(tail_q)
            emit_tail_b(tail_q, tail_qs, cc_ready)

        def make_ffn_bufs(stack, tag):
            fb = {}
            E = stack.enter_context
            for i in range(8):
                B[i] = Bank(E(nc.psum_tensor("pf%s%d" % (tag, i), [128, 512], F32)))
            fb["ln_fm"] = E(nc.sbuf_tensor("ln_fm" + tag, [128, 8 * MC], BF16))
            fb["y_acc"] = E(nc.sbuf_tensor("y_acc" + tag, [128, NTM * 1024], F32))
            fb["w13b"] = [E(nc.sbuf_tensor("w13_%s%d" % (tag, i), [128, 8 * 1024], BF16)) for i in range(2)]
            fb["w2b"] = [E(nc.sbuf_tensor("w2_%s%d" % (tag, i), [128, 4 * 1024], BF16)) for i in range(2)]
            fb["xsl"] = [E(nc.sbuf_tensor("xf%s%d" % (tag, i), [128, 1024], F32)) for i in range(2)]
            fb["ln_tm"] = [E(nc.sbuf_tensor("lt%s%d" % (tag, i), [128, 1024], BF16)) for i in range(2)]
            fb["h_sb"] = E(nc.sbuf_tensor("h" + tag, [128, 4 * MC], BF16))
            fb["sg"] = [E(nc.sbuf_tensor("sg%s%d" % (tag, i), [128, 512], BF16)) for i in range(2)]
            fb["sc_tm"] = E(nc.sbuf_tensor("sc" + tag, [128, 256], F32))
            fb["bt"] = [E(nc.sbuf_tensor("bt%s%d" % (tag, i), [128, 256], F32)) for i in range(2)]
            fb["msk"] = E(nc.sbuf_tensor("mk" + tag, [128, 256], F32))
            fb["gsum"] = E(nc.sbuf_tensor("gs" + tag, [128, 16], F32))
            fb["stt"] = E(nc.sbuf_tensor("stt" + tag, [128, 12], F32))
            fb["mv"] = E(nc.sbuf_tensor("mv" + tag, [128, 2 * NTM], F32))
            fb["rstd"] = E(nc.sbuf_tensor("rs" + tag, [128, NTM], F32))
            fb["xsl_free"] = [None, None]
            fb["ln_tm_free"] = [None, None]
            return fb

        with contextlib.ExitStack() as stack:
            fb = make_ffn_bufs(stack, "a")
            emit_ffn_pass(fb, x_in, x1)
            quiesce()
            P.flush()
            nc.all_engine_barrier()

        with contextlib.ExitStack() as stack:
            E = stack.enter_context
            ab = {}
            ab["K_fm"] = E(nc.sbuf_tensor("K_fm", [128, 8 * T], BF16))
            ab["v_tm"] = E(nc.sbuf_tensor("v_tm", [128, 32 * 1040], BF16))
            ab["xsl"] = [E(nc.sbuf_tensor("xat%d" % i, [128, 1024], F32)) for i in range(2)]
            ab["ln_tm"] = [E(nc.sbuf_tensor("ltat%d" % i, [128, 1024], BF16)) for i in range(2)]
            ab["stt"] = E(nc.sbuf_tensor("sttat", [128, 12], F32))
            ab["mv"] = E(nc.sbuf_tensor("mvat", [128, 8], F32))
            ab["rstd"] = E(nc.sbuf_tensor("rsat", [128, 4], F32))
            ab["xsl_free"] = [None, None]
            ab["ln_tm_free"] = [None, None]
            with contextlib.ExitStack() as s1:
                for i in range(8):
                    B[i] = Bank(s1.enter_context(nc.psum_tensor("pb1_%d" % i, [128, 512], F32)))
                ab["wkv_sb"] = s1.enter_context(nc.sbuf_tensor("wkv_sb", [128, 8 * 2048], BF16))
                ab["ln_fm"] = [s1.enter_context(nc.sbuf_tensor("lfa%d" % i, [128, 8 * 512], BF16)) for i in range(2)]
                emit_attn_b1(ab)
                quiesce()
                P.flush()
                nc.all_engine_barrier()
            with contextlib.ExitStack() as s2:
                E2c = s2.enter_context
                for i in range(8):
                    B[i] = Bank(E2c(nc.psum_tensor("pb2_%d" % i, [128, 512], F32)))
                ab["wq_sb"] = E2c(nc.sbuf_tensor("wq_sb", [128, 8 * 1024], BF16))
                ab["wo_sb"] = E2c(nc.sbuf_tensor("wo_sb", [128, 8 * 1024], BF16))
                ab["E2"] = [E2c(nc.sbuf_tensor("Eb%d" % i, [128, 16 * 128], BF16)) for i in range(4)]
                ab["q_sb"] = E2c(nc.sbuf_tensor("q_sb", [128, 8 * 128], BF16))
                ab["at_sb"] = E2c(nc.sbuf_tensor("at_sb", [128, 1024], BF16))
                ab["cc_sb"] = E2c(nc.sbuf_tensor("cc_sb", [128, 8 * 128], BF16))
                ab["ln_fmq"] = E2c(nc.sbuf_tensor("ln_fmq", [128, 8 * 128], BF16))
                ab["den"] = E2c(nc.sbuf_tensor("den", [128, 1], F32))
                emit_attn_b2(ab)
                quiesce()
                P.flush()
                nc.all_engine_barrier()

        with contextlib.ExitStack() as stack:
            fb = make_ffn_bufs(stack, "b")
            emit_ffn_pass(fb, x2, out)
            P.op("sync", lambda sy: sy.nop(), waits=[st2[0], st2[1]])
            quiesce()
            P.flush()
            nc.all_engine_barrier()

    return nc


_NC_CACHE = {}


def kernel(L0_attn_hidden_states, L0_attn_Wq, L0_attn_Wkv, L0_attn_Wo, L0_attn_bo,
           L0_ffn_input_ids, L0_ffn_gate_w, L0_ffn_gate_b, L0_ffn_w13, L0_ffn_w2,
           L0_ffn_w13_s, L0_ffn_w2_s):
    x = np.ascontiguousarray(np.asarray(L0_attn_hidden_states, dtype=np.float32))
    Wq = np.ascontiguousarray(np.asarray(L0_attn_Wq, dtype=np.float32))
    Wkv = np.ascontiguousarray(np.asarray(L0_attn_Wkv, dtype=np.float32))
    Wo = np.ascontiguousarray(np.asarray(L0_attn_Wo, dtype=np.float32))
    bo = np.asarray(L0_attn_bo, dtype=np.float32)
    gate_w = np.asarray(L0_ffn_gate_w, dtype=np.float32)
    gate_b = np.asarray(L0_ffn_gate_b, dtype=np.float32)
    w13 = np.asarray(L0_ffn_w13, dtype=np.float32)
    w2 = np.asarray(L0_ffn_w2, dtype=np.float32)
    w13_s = np.asarray(L0_ffn_w13_s, dtype=np.float32)
    w2_s = np.asarray(L0_ffn_w2_s, dtype=np.float32)

    w13t_ = np.ascontiguousarray(
        np.concatenate([w13.transpose(0, 2, 1), w13_s.T[None]], axis=0))
    w2t_ = np.ascontiguousarray(
        np.concatenate([w2.transpose(0, 2, 1), w2_s.T[None]], axis=0))
    gwt_ = np.ascontiguousarray(gate_w.T)
    gb_b_ = np.ascontiguousarray(
        np.broadcast_to(np.tile(gate_b, 16)[None, :], (128, 256)))
    bo_b_ = np.ascontiguousarray(np.broadcast_to(bo[None, :], (128, D)))

    if "nc" not in _NC_CACHE:
        _NC_CACHE["nc"] = build_program()
    nc = _NC_CACHE["nc"]

    in_maps = []
    for b in range(N_CORES):
        in_maps.append({
            "x_in": np.ascontiguousarray(x[b]),
            "w13t": w13t_, "w2t": w2t_, "gwt": gwt_, "gb_b": gb_b_,
            "wq": Wq, "wkv": Wkv, "wo": Wo, "bo_b": bo_b_,
        })
    res = run_bass_kernel_spmd(nc, in_maps, core_ids=list(range(N_CORES)))
    return np.stack([r["out"] for r in res.results], axis=0)


# version 09
# speedup vs baseline: 1.1381x; 1.1381x over previous
import contextlib
import os
import numpy as np
import concourse.bass as bass
import concourse.mybir as mybir
from concourse.ap import AP
from concourse.bass_utils import run_bass_kernel_spmd

F32 = mybir.dt.float32
BF16 = mybir.dt.bfloat16
AF = mybir.ActivationFunctionType
ALU = mybir.AluOpType
AX = mybir.AxisListType

T = 4096
D = 1024
H = 16
NEX = 17
MC = 1024
NMC = T // MC
NTM = MC // 128
NS = MC // 512
EPS = 1e-5
N_CORES = 8


class Sem:
    def __init__(self, nc, name):
        self.h = nc.alloc_semaphore(name)
        self.v = 0


class Bank:

    def __init__(self, t):
        self.t = t
        self.free = None

    def w(self):
        return [self.free] if self.free else []


class Prog:
    ENGINES = ("tensor", "vector", "scalar", "gpsimd", "sync")

    def __init__(self, nc):
        self.nc = nc
        self.ops = {e: [] for e in self.ENGINES}
        self.hw = {e: {} for e in self.ENGINES}

    def op(self, eng, fn, waits=(), inc=None, incn=1):
        ws = []
        for w in waits:
            if w is None:
                continue
            if isinstance(w, tuple):
                s, v = w
            else:
                s, v = w, w.v
            if v <= 0:
                continue
            ws.append((s.h, v))
        hwe = self.hw[eng]
        ws2 = []
        for (h, v) in ws:
            key = h.num
            if hwe.get(key, -1) < v:
                hwe[key] = v
                ws2.append((h, v))
        if inc is not None:
            inc.v += incn

        def run(e, fn=fn, ws2=ws2, inc=inc, incn=incn):
            for (h, v) in ws2:
                e.wait_ge(h, v)
            inst = fn(e)
            if inc is not None:
                inst.then_inc(inc.h, incn)

        self.ops[eng].append(run)

    def flush(self):
        with self.nc.Block() as blk:
            for name in self.ENGINES:
                if self.ops[name]:
                    getattr(blk, name)(
                        lambda eng, _n=name: [f(eng) for f in self.ops[_n]])
        self.ops = {e: [] for e in self.ENGINES}
        self.hw = {e: {} for e in self.ENGINES}


def build_program():
    nc = bass.Bass("TRN2", target_bir_lowering=False)

    x_in = nc.dram_tensor("x_in", [T, D], F32, kind="ExternalInput").ap()
    w13t = nc.dram_tensor("w13t", [NEX, D, D], F32, kind="ExternalInput").ap()
    w2t = nc.dram_tensor("w2t", [NEX, 512, D], F32, kind="ExternalInput").ap()
    gwt = nc.dram_tensor("gwt", [D, 16], F32, kind="ExternalInput").ap()
    gb_b = nc.dram_tensor("gb_b", [128, NTM * 16], F32, kind="ExternalInput").ap()
    wq = nc.dram_tensor("wq", [D, D], F32, kind="ExternalInput").ap()
    wkv = nc.dram_tensor("wkv", [D, 2 * D], F32, kind="ExternalInput").ap()
    wo = nc.dram_tensor("wo", [D, D], F32, kind="ExternalInput").ap()
    bo_b = nc.dram_tensor("bo_b", [128, D], F32, kind="ExternalInput").ap()
    out = nc.dram_tensor("out", [T, D], F32, kind="ExternalOutput").ap()
    _tap = "ExternalOutput" if os.environ.get("KMK_DEBUG_TAPS") else "Internal"
    x1 = nc.dram_tensor("x1s", [T, D], F32, kind=_tap).ap()
    x2 = nc.dram_tensor("x2s", [T, D], F32, kind=_tap).ap()

    with nc.cleanup_on_exit():
        ident_bf = nc.alloc_sbuf_tensor("ident_bf", [128, 128], BF16)
        ident_f = nc.alloc_sbuf_tensor("ident_f", [128, 128], F32)
        consts = nc.alloc_sbuf_tensor("consts", [128, 4], F32)
        gbb_sb = nc.alloc_sbuf_tensor("gbb_sb", [128, NTM * 16], F32)
        bob_sb = nc.alloc_sbuf_tensor("bob_sb", [128, D], F32)
        gwt_sb = nc.alloc_sbuf_tensor("gwt_sb", [128, 8 * 16], BF16)
        B = [None] * 8

        mm = Sem(nc, "mm"); ev = Sem(nc, "ev"); es = Sem(nc, "es")
        gv = Sem(nc, "gv"); ldi = Sem(nc, "ldi")
        ldw2 = [Sem(nc, "ldwa"), Sem(nc, "ldwb")]
        lx2 = [Sem(nc, "lxa"), Sem(nc, "lxb")]
        st2 = [Sem(nc, "sta"), Sem(nc, "stb")]

        P = Prog(nc)

        P.op("gpsimd", lambda g: g.memset(ident_bf[:], 0.0), waits=[gv], inc=gv)
        P.op("gpsimd", lambda g: g.affine_select(
            out=ident_bf[:], in_=ident_bf[:], compare_op=ALU.not_equal,
            fill=1.0, base=0, pattern=[[-1, 128]], channel_multiplier=1),
            waits=[gv], inc=gv)
        P.op("gpsimd", lambda g: g.memset(ident_f[:], 0.0), waits=[gv], inc=gv)
        P.op("gpsimd", lambda g: g.affine_select(
            out=ident_f[:], in_=ident_f[:], compare_op=ALU.not_equal,
            fill=1.0, base=0, pattern=[[-1, 128]], channel_multiplier=1),
            waits=[gv], inc=gv)
        P.op("gpsimd", lambda g: g.memset(consts[:, 0:1], EPS), waits=[gv], inc=gv)
        P.op("gpsimd", lambda g: g.memset(consts[:, 1:2], -0.5), waits=[gv], inc=gv)
        P.op("gpsimd", lambda g: g.memset(consts[:, 2:3], 0.125), waits=[gv], inc=gv)
        P.op("gpsimd", lambda g: g.dma_start(gbb_sb[:], gb_b), inc=ldi, incn=16)
        P.op("gpsimd", lambda g: g.dma_start(bob_sb[:], bo_b), inc=ldi, incn=16)
        P.op("gpsimd", lambda g: g.dma_start(
            gwt_sb[:], gwt.rearrange("(kc p) e -> p kc e", p=128)), inc=ldi, incn=16)
        init_snap = (ldi, ldi.v)
        gv_snap = (gv, gv.v)

        def quiesce():
            P.op("gpsimd", lambda g: g.nop(),
                 waits=[ldw2[0], ldw2[1], lx2[0], lx2[1], st2[0], st2[1], ldi])

        def emit_ln(src, row0, ntiles, xsl, xsl_free, lxs, ln_tm, ln_tm_free,
                    stt, mv, rstd, ln_fm_sb, fm_cols, tp_banks, dst_war=None,
                    keep_x=None):
            for ti in range(ntiles):
                slot = xsl[ti % len(xsl)]
                lsem = lxs[ti % len(xsl)]
                P.op("sync", lambda sy, s=slot, r=row0 + ti * 128:
                     sy.dma_start(s[:], src[r:r + 128, :]),
                     waits=[xsl_free[ti % len(xsl)]], inc=lsem, incn=16)
                ld_snap = (lsem, lsem.v)
                P.op("vector", lambda v, s=slot: v.bn_stats(stt[:, 0:6], s[:, 0:512]),
                     waits=[ld_snap, ev])
                P.op("vector", lambda v, s=slot: v.bn_stats(stt[:, 6:12], s[:, 512:1024]),
                     inc=ev)
                P.op("vector", lambda v, ti=ti: v.bn_aggr(mv[:, 2 * ti:2 * ti + 2], stt[:]),
                     waits=[ev], inc=ev)
                P.op("scalar", lambda s, ti=ti: s.activation(
                    rstd[:, ti:ti + 1], mv[:, 2 * ti + 1:2 * ti + 2], AF.Ln,
                    bias=consts[:, 0:1]), waits=[ev, es, gv_snap], inc=es)
                P.op("scalar", lambda s, ti=ti: s.activation(
                    rstd[:, ti:ti + 1], rstd[:, ti:ti + 1], AF.Exp,
                    scale=consts[:, 1:2]), waits=[es], inc=es)
                tslot = ln_tm[ti % len(ln_tm)]
                P.op("vector", lambda v, s=slot, t=tslot, ti=ti: v.tensor_scalar(
                    t[:], s[:], mv[:, 2 * ti:2 * ti + 1], rstd[:, ti:ti + 1],
                    ALU.subtract, ALU.mult),
                    waits=[es, ev, ln_tm_free[ti % len(ln_tm)]], inc=ev)
                apply_snap = (ev, ev.v)
                if keep_x is not None:
                    keep_x[ti] = slot
                else:
                    xsl_free[ti % len(xsl)] = apply_snap
                for bk in range(2):
                    bank = tp_banks[bk]
                    P.op("tensor", lambda t, tt=tslot, b=bank.t, bk=bk: [
                        t.matmul(b[:, j * 128:(j + 1) * 128],
                                 tt[:, (bk * 4 + j) * 128:(bk * 4 + j + 1) * 128],
                                 ident_bf[:], start=True, stop=True)
                        for j in range(4)][-1],
                        waits=[apply_snap, gv_snap] + bank.w(), inc=mm)
                    mmsnap = (mm, mm.v)
                    P.op("vector", lambda v, b=bank.t, ti=ti, bk=bk: v.tensor_copy(
                        AP(tensor=ln_fm_sb.ap().tensor,
                           offset=(bk * 4) * fm_cols + ti * 128,
                           ap=[ln_fm_sb.ap().ap[0], [fm_cols, 4], [1, 128]]),
                        b[:, 0:512]),
                        waits=[mmsnap, dst_war], inc=ev)
                    bank.free = (ev, ev.v)
                ln_tm_free[ti % len(ln_tm)] = (mm, mm.v)
            return (ev, ev.v)

        def emit_ffn_pass(fb, src, dst):
            ln_fm = fb["ln_fm"]; y_acc = fb["y_acc"]
            w13b = fb["w13b"]; w2b = fb["w2b"]
            xsl = fb["xsl"]; ln_tm = fb["ln_tm"]
            h_sb = fb["h_sb"]; sg = fb["sg"]
            sc_tm = fb["sc_tm"]; bt = fb["bt"]; msk = fb["msk"]; gsum = fb["gsum"]
            stt = fb["stt"]; mv = fb["mv"]; rstd = fb["rstd"]

            w13_free = [None, None]; w2_free = [None, None]
            xsl_free = fb["xsl_free"]
            ln_tm_free = fb["ln_tm_free"]
            sg_free = [None, None]
            h_war = {}
            yacc_free = None
            lnfm_war = None

            def preload(idx):
                e = idx % NEX
                buf = idx % 2
                lsem = ldw2[buf]
                P.op("gpsimd", lambda g, e=e, b=buf: g.dma_start(
                    w13b[b][:], w13t[e].rearrange("(kc p) j -> p kc j", p=128)),
                    waits=[w13_free[buf]], inc=lsem, incn=16)
                P.op("gpsimd", lambda g, e=e, b=buf: g.dma_start(
                    w2b[b][:], w2t[e].rearrange("(ic p) d -> p ic d", p=128)),
                    waits=[w2_free[buf]], inc=lsem, incn=16)
                return (lsem, lsem.v)

            w_snaps = {0: preload(0)}
            sidx = 0
            gu_banks = [B[0], B[1], B[2]]
            y_banks = [B[3], B[4], B[5], B[6], B[7]]
            gu_i = [0]; y_i = [0]

            for mc in range(NMC):
                lnfm_ready = emit_ln(src, mc * MC, NTM, xsl, xsl_free, lx2, ln_tm,
                                     ln_tm_free, stt, mv, rstd, ln_fm, MC,
                                     [B[0], B[1]], dst_war=lnfm_war)
                gate_bank = B[2]
                for half in range(NTM // 8):
                    P.op("tensor", lambda t, half=half, b=gate_bank.t: [
                        t.matmul(b[:, tj * 16:tj * 16 + 16],
                                 ln_fm[:, kc * MC + (half * 8 + tj) * 128:
                                       kc * MC + (half * 8 + tj) * 128 + 128],
                                 gwt_sb[:, kc * 16:kc * 16 + 16],
                                 start=(kc == 0), stop=(kc == 7))
                        for tj in range(8) for kc in range(8)][-1],
                        waits=[lnfm_ready, init_snap] + gate_bank.w(), inc=mm)
                    mmsnap = (mm, mm.v)
                    P.op("scalar", lambda s, half=half, b=gate_bank.t: s.activation(
                        sc_tm[:, half * 128:half * 128 + 128], b[:, 0:128],
                        AF.Sigmoid), waits=[mmsnap], inc=es)
                    gate_bank.free = (es, es.v)

                GE = 16
                P.op("vector", lambda v: v.tensor_add(bt[0][:], sc_tm[:], gbb_sb[:]),
                     waits=[es, ev], inc=ev)

                def lanes_ap(tens, fixed):
                    base = tens.ap()
                    off = sum(b_ for b_, val in fixed.items() if val)
                    dims = [[GE, NTM]]
                    for b_ in (8, 4, 2, 1):
                        if b_ not in fixed:
                            dims.append([b_, 2])
                    merged = [dims[0]]
                    for d_ in dims[1:]:
                        o = merged[-1]
                        if o[0] == d_[0] * d_[1]:
                            merged[-1] = [d_[0], d_[1] * o[1]]
                        else:
                            merged.append(d_)
                    return AP(tensor=base.tensor, offset=off,
                              ap=[base.ap[0]] + merged)

                srcb, dstb = bt[0], bt[1]
                for k in (2, 4, 8, 16):
                    j = k // 2
                    while j >= 1:
                        pairs = [({j: 0}, {j: 1}, ALU.max, ALU.min)] if k == 16 else [
                            ({j: 0, k: 0}, {j: 1, k: 0}, ALU.max, ALU.min),
                            ({j: 0, k: 1}, {j: 1, k: 1}, ALU.min, ALU.max)]
                        first = True
                        for (lo, hi, o1, o2) in pairs:
                            P.op("vector", lambda v, s=srcb, d2=dstb, lo=lo, hi=hi, o1=o1:
                                 v.tensor_tensor(lanes_ap(d2, lo), lanes_ap(s, lo),
                                                 lanes_ap(s, hi), o1),
                                 waits=([ev] if first else []))
                            first = False
                            P.op("vector", lambda v, s=srcb, d2=dstb, lo=lo, hi=hi, o2=o2:
                                 v.tensor_tensor(lanes_ap(d2, hi), lanes_ap(s, lo),
                                                 lanes_ap(s, hi), o2),
                                 inc=ev)
                        srcb, dstb = dstb, srcb
                        j //= 2
                P.op("vector", lambda v, d2=dstb: v.tensor_add(d2[:], sc_tm[:], gbb_sb[:]),
                     waits=[ev], inc=ev)
                theta = AP(tensor=srcb.ap().tensor, offset=3,
                           ap=[srcb.ap().ap[0], [GE, NTM], [0, GE]])

                def g3(t_):
                    return AP(tensor=t_.ap().tensor, offset=0,
                              ap=[t_.ap().ap[0], [GE, NTM], [1, GE]])

                P.op("vector", lambda v, d2=dstb: v.tensor_tensor(
                    g3(msk), g3(d2), theta, ALU.is_ge), waits=[ev], inc=ev)
                P.op("vector", lambda v: v.tensor_mul(msk[:], sc_tm[:], msk[:]),
                     waits=[ev], inc=ev)
                P.op("vector", lambda v: v.reduce_sum(gsum[:], g3(msk), AX.X),
                     waits=[ev], inc=ev)
                P.op("vector", lambda v: v.reciprocal(gsum[:], gsum[:]), waits=[ev], inc=ev)
                P.op("vector", lambda v: v.tensor_scalar(
                    gsum[:], gsum[:], 2.5, None, ALU.mult), waits=[ev], inc=ev)
                rb = AP(tensor=gsum.ap().tensor, offset=0,
                        ap=[gsum.ap().ap[0], [1, NTM], [0, GE]])
                P.op("vector", lambda v: v.tensor_tensor(g3(msk), g3(msk), rb, ALU.mult),
                     waits=[ev], inc=ev)
                msk_ready = (ev, ev.v)

                for e in range(NEX):
                    buf = sidx % 2
                    w_snap = w_snaps[sidx]
                    w_snaps[sidx + 1] = preload(sidx + 1)
                    is_shared = (e == NEX - 1)

                    h_readys = {}

                    def emit_y(s):
                            for tsub in range(4):
                                cc = (s * 4 + tsub) * 16 + e
                                for dhf in range(2):
                                    yi = y_i[0]; y_i[0] += 1
                                    bank = y_banks[yi % 5]
                                    P.op("tensor", lambda t, s=s, tsub=tsub, dhf=dhf,
                                         b=buf, bk=bank.t: [
                                             t.matmul(bk[:, 0:512],
                                                      h_sb[:, ic * MC + s * 512 + tsub * 128:
                                                            ic * MC + s * 512 + tsub * 128 + 128],
                                                      w2b[b][:, ic * 1024 + dhf * 512:
                                                            ic * 1024 + dhf * 512 + 512],
                                                      start=(ic == 0), stop=(ic == 3))
                                             for ic in range(4)][-1],
                                        waits=[h_readys[s], w_snap] + bank.w(), inc=mm)
                                    mmsnap = (mm, mm.v)
                                    ycol = (s * 4 + tsub) * 1024 + dhf * 512
                                    if is_shared:
                                        P.op("vector", lambda v, ycol=ycol, bk=bank.t:
                                             v.tensor_tensor(y_acc[:, ycol:ycol + 512],
                                                             bk[:, 0:512],
                                                             y_acc[:, ycol:ycol + 512],
                                                             ALU.add),
                                             waits=[mmsnap, ev], inc=ev)
                                    elif e == 0:
                                        P.op("vector", lambda v, ycol=ycol, bk=bank.t,
                                             cc=cc: v.tensor_scalar(
                                                 y_acc[:, ycol:ycol + 512], bk[:, 0:512],
                                                 msk[:, cc:cc + 1], None, ALU.mult),
                                             waits=[mmsnap, msk_ready, yacc_free], inc=ev)
                                    else:
                                        P.op("vector", lambda v, ycol=ycol, bk=bank.t,
                                             cc=cc: v.scalar_tensor_tensor(
                                                 y_acc[:, ycol:ycol + 512], bk[:, 0:512],
                                                 msk[:, cc:cc + 1],
                                                 y_acc[:, ycol:ycol + 512],
                                                 ALU.mult, ALU.add),
                                             waits=[mmsnap, ev], inc=ev)
                                    bank.free = (ev, ev.v)
                            h_war[s] = (mm, mm.v)

                    for s in range(NS):
                        sg_snap = None
                        for pair in range(4):
                            sslot = pair % 2
                            for gate_part in (True, False):
                                jg = pair if gate_part else pair + 4
                                gi = gu_i[0]; gu_i[0] += 1
                                bank = gu_banks[gi % 3]
                                P.op("tensor", lambda t, jg=jg, s=s, b=buf, bk=bank.t: [
                                    t.matmul(bk[:, 0:512],
                                             w13b[b][:, kc * 1024 + jg * 128:
                                                   kc * 1024 + jg * 128 + 128],
                                             ln_fm[:, kc * MC + s * 512:
                                                   kc * MC + s * 512 + 512],
                                             start=(kc == 0), stop=(kc == 7))
                                    for kc in range(8)][-1],
                                    waits=[w_snap, lnfm_ready] + bank.w(), inc=mm)
                                mmsnap = (mm, mm.v)
                                if gate_part:
                                    P.op("scalar", lambda sc, sslot=sslot, bk=bank.t:
                                         sc.activation(sg[sslot][:], bk[:, 0:512],
                                                       AF.Silu),
                                         waits=[mmsnap, sg_free[sslot]], inc=es)
                                    sg_snap = (es, es.v)
                                    bank.free = (es, es.v)
                                else:
                                    ic = pair
                                    hcol = ic * MC + s * 512
                                    hw_ = h_war.get(s)
                                    P.op("vector", lambda v, hcol=hcol, bk=bank.t,
                                         sslot=sslot: v.tensor_tensor(
                                             h_sb[:, hcol:hcol + 512],
                                             bk[:, 0:512], sg[sslot][:],
                                             ALU.mult),
                                         waits=[mmsnap, sg_snap, hw_], inc=ev)
                                    bank.free = (ev, ev.v)
                                    sg_free[sslot] = (ev, ev.v)
                        h_readys[s] = (ev, ev.v)
                        if s >= 1:
                            emit_y(s - 1)
                    emit_y(NS - 1)
                    w13_free[buf] = (mm, mm.v)
                    w2_free[buf] = (mm, mm.v)
                    sidx += 1
                lnfm_war = (mm, mm.v)
                y_done = (ev, ev.v)

                for ti in range(NTM):
                    slot = xsl[ti % 2]
                    lsem = lx2[ti % 2]
                    ssem = st2[ti % 2]
                    P.op("sync", lambda sy, s=slot, r=mc * MC + ti * 128:
                         sy.dma_start(s[:], src[r:r + 128, :]),
                         waits=[xsl_free[ti % 2]], inc=lsem, incn=16)
                    ld_snap = (lsem, lsem.v)
                    P.op("vector", lambda v, s=slot, ti=ti:
                         v.scalar_tensor_tensor(
                             s[:], y_acc[:, ti * 1024:ti * 1024 + 1024], 0.5, s[:],
                             ALU.mult, ALU.add),
                         waits=[ld_snap, y_done], inc=ev)
                    evsnap = (ev, ev.v)
                    P.op("sync", lambda sy, s=slot, r=mc * MC + ti * 128:
                         sy.dma_start(dst[r:r + 128, :], s[:]),
                         waits=[evsnap], inc=ssem, incn=16)
                    xsl_free[ti % 2] = (ssem, ssem.v)
                yacc_free = (ev, ev.v)

        def emit_attn_b1(ab):
            K_fm = ab["K_fm"]; v_tm = ab["v_tm"]; wkv_sb = ab["wkv_sb"]
            ln_fm2 = ab["ln_fm"]; ln_tm = ab["ln_tm"]; xsl = ab["xsl"]
            stt = ab["stt"]; mv = ab["mv"]; rstd = ab["rstd"]
            xsl_free = ab["xsl_free"]; ln_tm_free = ab["ln_tm_free"]
            lnfm_war = [None, None]

            P.op("gpsimd", lambda g: g.dma_start(
                wkv_sb[:], wkv.rearrange("(kc p) n -> p kc n", p=128)),
                inc=ldi, incn=16)
            wkv_snap = (ldi, ldi.v)
            P.op("vector", lambda v: v.memset(v_tm[:], 1.0), waits=[ev], inc=ev)
            vmemset = (ev, ev.v)

            kb = [B[2], B[3]]
            vb = [B[4], B[5]]
            for c in range(8):
                lf = c % 2
                ln_ready = emit_ln(x1, c * 512, 4, xsl, xsl_free, lx2, ln_tm,
                                   ln_tm_free, stt, mv, rstd, ln_fm2[lf], 512,
                                   [B[0], B[1]], dst_war=lnfm_war[lf])
                for ic in range(8):
                    bank = kb[ic % 2]
                    P.op("tensor", lambda t, ic=ic, lf=lf, bk=bank.t: [
                        t.matmul(bk[:, 0:512],
                                 wkv_sb[:, kc * 2048 + ic * 128:kc * 2048 + ic * 128 + 128],
                                 ln_fm2[lf][:, kc * 512:kc * 512 + 512],
                                 start=(kc == 0), stop=(kc == 7))
                        for kc in range(8)][-1],
                        waits=[ln_ready, wkv_snap] + bank.w(), inc=mm)
                    mmsnap = (mm, mm.v)
                    P.op("scalar", lambda s, ic=ic, c=c, bk=bank.t: s.copy(
                        K_fm[:, ic * T + c * 512:ic * T + c * 512 + 512],
                        bk[:, 0:512]), waits=[mmsnap], inc=es)
                    bank.free = (es, es.v)
                for tsub in range(4):
                    for vh in range(2):
                        gi = tsub * 2 + vh
                        bank = vb[gi % 2]
                        P.op("tensor", lambda t, tsub=tsub, vh=vh, lf=lf, bk=bank.t: [
                            t.matmul(bk[:, 0:512],
                                     ln_fm2[lf][:, kc * 512 + tsub * 128:
                                                kc * 512 + tsub * 128 + 128],
                                     wkv_sb[:, kc * 2048 + 1024 + vh * 512:
                                            kc * 2048 + 1024 + vh * 512 + 512],
                                     start=(kc == 0), stop=(kc == 7))
                            for kc in range(8)][-1],
                            waits=[ln_ready, wkv_snap] + bank.w(), inc=mm)
                        mmsnap = (mm, mm.v)
                        row = (c * 4 + tsub) * 1040 + vh * 8 * 65
                        P.op("vector", lambda v, row=row, bk=bank.t: v.tensor_copy(
                            AP(tensor=v_tm.ap().tensor, offset=row,
                               ap=[v_tm.ap().ap[0], [65, 8], [1, 64]]),
                            bk[:, 0:512]), waits=[mmsnap, vmemset], inc=ev)
                        bank.free = (ev, ev.v)
                lnfm_war[lf] = (mm, mm.v)

        def emit_attn_b2(ab):
            K_fm = ab["K_fm"]; v_tm = ab["v_tm"]
            wq_sb = ab["wq_sb"]; wo_sb = ab["wo_sb"]
            E2 = ab["E2"]; q_sb = ab["q_sb"]; at_sb = ab["at_sb"]; cc_sb = ab["cc_sb"]
            xsl = ab["xsl"]; ln_tm = ab["ln_tm"]; den = ab["den"]
            stt = ab["stt"]; mv = ab["mv"]; rstd = ab["rstd"]
            ln_fmq = ab["ln_fmq"]
            xsl_free = ab["xsl_free"]; ln_tm_free = ab["ln_tm_free"]

            P.op("gpsimd", lambda g: g.dma_start(
                wq_sb[:], wq.rearrange("(kc p) n -> p kc n", p=128)), inc=ldi, incn=16)
            P.op("gpsimd", lambda g: g.dma_start(
                wo_sb[:], wo.rearrange("(kc p) n -> p kc n", p=128)), inc=ldi, incn=16)
            w_snap = (ldi, ldi.v)

            E_free = [None, None, None, None]
            proj = [B[2], B[3]]
            scb = [B[4], B[5]]
            scb2 = [B[6], B[7]]
            pvb = B[0]
            pvb2 = B[1]
            atb = [at_sb, ln_tm[1]]
            st_ctx = {"at_war": [None, None], "cc_war": None, "lnfmq_war": None,
                      "at": at_sb, "atq": 0}

            def emit_pv_half(h, pvbk, exp_sn, ebufs, half):
                eb = E2[ebufs[half]]
                for gg in range(4):
                    g = half * 4 + gg
                    P.op("tensor", lambda t, g=g, gg=gg, h=h, eb=eb, bk=pvbk.t: [
                        t.matmul(bk[:, 0:65],
                                 eb[:, (gg * 4 + cj) * 128:(gg * 4 + cj) * 128 + 128],
                                 v_tm[:, (g * 4 + cj) * 1040 + h * 65:
                                       (g * 4 + cj) * 1040 + h * 65 + 65],
                                 start=(g == 0 and cj == 0),
                                 stop=(g == 7 and cj == 3))
                        for cj in range(4)][-1],
                        waits=[exp_sn[g]] + (pvbk.w() if g == 0 else []),
                        inc=mm)
                E_free[ebufs[half]] = (mm, mm.v)
                if half == 1:
                    pv_done = (mm, mm.v)
                    P.op("vector", lambda v, bk=pvbk.t: v.reciprocal(
                        den[:, 0:1], bk[:, 64:65]), waits=[pv_done, ev], inc=ev)
                    P.op("vector", lambda v, h=h, a=st_ctx["at"], bk=pvbk.t:
                         v.tensor_scalar(
                             a[:, h * 64:h * 64 + 64], bk[:, 0:64], den[:, 0:1],
                             None, ALU.mult),
                         waits=[ev] + ([st_ctx["at_war"][st_ctx["atq"] % 2]]
                                       if h == 0 else []), inc=ev)
                    pvbk.free = (ev, ev.v)

            def emit_qk_g(g, ic, exp_snaps, h0, h1, q_ready):
                bank0 = scb[g % 2]
                bank1 = scb2[g % 2]
                P.op("tensor", lambda t, g=g, ic=ic, b0=bank0.t, b1=bank1.t: [
                    t.matmul((b0 if hh == 0 else b1)[:, cj * 128:cj * 128 + 128],
                             K_fm.ap()[hh * 64:hh * 64 + 64,
                                       ic * T + (g * 4 + cj) * 128:
                                       ic * T + (g * 4 + cj) * 128 + 128],
                             q_sb.ap()[hh * 64:hh * 64 + 64,
                                       ic * 128:ic * 128 + 128],
                             start=True, stop=True, tile_position=(hh * 64, 0))
                    for cj in range(4) for hh in (0, 1)][-1],
                    waits=[q_ready] + bank0.w() + bank1.w(), inc=mm)
                mmsnap = (mm, mm.v)
                ecol = (g % 4) * 512
                for (h, bank, ebi) in ((h0, bank0, 0 if g < 4 else 1),
                                       (h1, bank1, 2 if g < 4 else 3)):
                    P.op("scalar", lambda s, ecol=ecol, eb=E2[ebi], bk=bank.t:
                         s.activation(eb[:, ecol:ecol + 512], bk[:, 0:512], AF.Exp),
                         waits=[mmsnap, E_free[ebi]], inc=es)
                    exp_snaps[h].append((es, es.v))
                    bank.free = (es, es.v)

            def prep_q(q):
                keep = {}
                sub_free = [xsl_free[q % 2]]
                ln_ready = emit_ln(x1, q * 128, 1, [xsl[q % 2]], sub_free,
                                   [lx2[q % 2]], ln_tm, ln_tm_free,
                                   stt, mv, rstd, ln_fmq, 128, [proj[0], proj[1]],
                                   dst_war=st_ctx["lnfmq_war"], keep_x=keep)
                for bk2 in range(2):
                    bank = proj[bk2]
                    P.op("tensor", lambda t, bk2=bk2, bk=bank.t: [
                        t.matmul(bk[:, gj * 128:gj * 128 + 128],
                                 wq_sb[:, kc * 1024 + (bk2 * 4 + gj) * 128:
                                       kc * 1024 + (bk2 * 4 + gj) * 128 + 128],
                                 ln_fmq[:, kc * 128:kc * 128 + 128],
                                 start=(kc == 0), stop=(kc == 7))
                        for gj in range(4) for kc in range(8)][-1],
                        waits=[ln_ready, w_snap] + bank.w(), inc=mm)
                st_ctx["lnfmq_war"] = (mm, mm.v)
                return {"xq": keep[0], "qmm": (mm, mm.v)}

            def emit_qevac(qs):
                for bk2 in range(2):
                    bank = proj[bk2]
                    P.op("scalar", lambda sc, bk2=bk2, bk=bank.t: sc.activation(
                        q_sb[:, bk2 * 512:bk2 * 512 + 512], bk[:, 0:512], AF.Copy,
                        scale=consts[:, 2:3]), waits=[qs["qmm"], gv_snap, es], inc=es)
                    bank.free = (es, es.v)
                return (es, es.v)

            def emit_tail_a(q):
                a = atb[q % 2]
                at_ready = (ev, ev.v)
                for rnd in range(4):
                    bank = (scb + scb2)[rnd]
                    P.op("tensor", lambda t, rnd=rnd, a=a, bk=bank.t: [
                        t.matmul(bk.ap()[0:64, j * 128:j * 128 + 128],
                                 a[:, (rnd * 4 + j) * 64:(rnd * 4 + j + 1) * 64],
                                 ident_bf[:], start=True, stop=True)
                        for j in range(4)][-1],
                        waits=[at_ready, gv_snap] + bank.w(), inc=mm)
                    mmsnap = (mm, mm.v)
                    for j in range(4):
                        hh = rnd * 4 + j
                        P.op("vector", lambda v, hh=hh, j=j, bk=bank.t: v.tensor_copy(
                            cc_sb.ap()[(hh % 2) * 64:(hh % 2) * 64 + 64,
                                       (hh // 2) * 128:(hh // 2) * 128 + 128],
                            bk.ap()[0:64, j * 128:j * 128 + 128]),
                            waits=[mmsnap] + ([st_ctx["cc_war"]] if (rnd == 0 and j == 0) else []),
                            inc=ev)
                    bank.free = (ev, ev.v)
                st_ctx["at_war"][q % 2] = (mm, mm.v)
                return (ev, ev.v)

            def emit_tail_b(q, qs, cc_ready):
                xq = qs["xq"]
                P.op("vector", lambda v, xq=xq: v.tensor_tensor(
                    xq[:], xq[:], bob_sb[:], ALU.add), waits=[ev], inc=ev)
                for dhf in range(2):
                    bank = proj[dhf]
                    P.op("tensor", lambda t, dhf=dhf, bk=bank.t: [
                        t.matmul(bk[:, 0:512],
                                 cc_sb[:, ic2 * 128:ic2 * 128 + 128],
                                 wo_sb[:, ic2 * 1024 + dhf * 512:
                                       ic2 * 1024 + dhf * 512 + 512],
                                 start=(ic2 == 0), stop=(ic2 == 7))
                        for ic2 in range(8)][-1],
                        waits=[cc_ready, w_snap] + bank.w(), inc=mm)
                    mmsnap = (mm, mm.v)
                    P.op("vector", lambda v, dhf=dhf, xq=xq, bk=bank.t:
                         v.tensor_tensor(xq[:, dhf * 512:dhf * 512 + 512],
                                         bk[:, 0:512],
                                         xq[:, dhf * 512:dhf * 512 + 512],
                                         ALU.add),
                         waits=[mmsnap, ev], inc=ev)
                    bank.free = (ev, ev.v)
                st_ctx["cc_war"] = (mm, mm.v)
                evsnap = (ev, ev.v)
                ssem = st2[q % 2]
                P.op("sync", lambda sy, xq=xq, q=q, ssem=ssem: sy.dma_start(
                    x2[q * 128:(q + 1) * 128, :], xq[:]), waits=[evsnap],
                    inc=ssem, incn=16)
                xsl_free[q % 2] = (ssem, ssem.v)

            qs_cur = prep_q(0)
            q_ready = emit_qevac(qs_cur)
            qs_next = None
            tail_q = None
            tail_qs = None
            cc_ready = None
            for q in range(32):
                st_ctx["at"] = atb[q % 2]
                st_ctx["atq"] = q
                pend = {}
                for hp in range(H // 2):
                    h0, h1 = 2 * hp, 2 * hp + 1
                    ic = hp
                    exp_snaps = {h0: [], h1: []}
                    if "h1A" in pend:
                        emit_pv_half(*pend.pop("h1A"), 0)
                    for g in range(4):
                        emit_qk_g(g, ic, exp_snaps, h0, h1, q_ready)
                    if "h0B" in pend:
                        emit_pv_half(*pend.pop("h0B"), 1)
                    if "h1B" in pend:
                        emit_pv_half(*pend.pop("h1B"), 1)
                    for g in range(4, 8):
                        emit_qk_g(g, ic, exp_snaps, h0, h1, q_ready)
                    if hp == 2 and tail_q is not None:
                        cc_ready = emit_tail_a(tail_q)
                    if hp == 3 and tail_q is not None:
                        emit_tail_b(tail_q, tail_qs, cc_ready)
                        tail_q = None
                    if hp == 4 and q + 1 < 32:
                        qs_next = prep_q(q + 1)
                    emit_pv_half(h0, pvb, exp_snaps[h0], (0, 1), 0)
                    pend["h0B"] = (h0, pvb, exp_snaps[h0], (0, 1))
                    pend["h1A"] = (h1, pvb2, exp_snaps[h1], (2, 3))
                    pend["h1B"] = (h1, pvb2, exp_snaps[h1], (2, 3))
                emit_pv_half(*pend.pop("h1A"), 0)
                emit_pv_half(*pend.pop("h0B"), 1)
                emit_pv_half(*pend.pop("h1B"), 1)
                tail_q = q
                tail_qs = qs_cur
                if q + 1 < 32:
                    q_ready = emit_qevac(qs_next)
                    qs_cur = qs_next
            cc_ready = emit_tail_a(tail_q)
            emit_tail_b(tail_q, tail_qs, cc_ready)

        def make_ffn_bufs(stack, tag):
            fb = {}
            E = stack.enter_context
            for i in range(8):
                B[i] = Bank(E(nc.psum_tensor("pf%s%d" % (tag, i), [128, 512], F32)))
            fb["ln_fm"] = E(nc.sbuf_tensor("ln_fm" + tag, [128, 8 * MC], BF16))
            fb["y_acc"] = E(nc.sbuf_tensor("y_acc" + tag, [128, NTM * 1024], F32))
            fb["w13b"] = [E(nc.sbuf_tensor("w13_%s%d" % (tag, i), [128, 8 * 1024], BF16)) for i in range(2)]
            fb["w2b"] = [E(nc.sbuf_tensor("w2_%s%d" % (tag, i), [128, 4 * 1024], BF16)) for i in range(2)]
            fb["xsl"] = [E(nc.sbuf_tensor("xf%s%d" % (tag, i), [128, 1024], F32)) for i in range(2)]
            fb["ln_tm"] = [E(nc.sbuf_tensor("lt%s%d" % (tag, i), [128, 1024], BF16)) for i in range(2)]
            fb["h_sb"] = E(nc.sbuf_tensor("h" + tag, [128, 4 * MC], BF16))
            fb["sg"] = [E(nc.sbuf_tensor("sg%s%d" % (tag, i), [128, 512], BF16)) for i in range(2)]
            fb["sc_tm"] = E(nc.sbuf_tensor("sc" + tag, [128, NTM * 16], F32))
            fb["bt"] = [E(nc.sbuf_tensor("bt%s%d" % (tag, i), [128, NTM * 16], F32)) for i in range(2)]
            fb["msk"] = E(nc.sbuf_tensor("mk" + tag, [128, NTM * 16], F32))
            fb["gsum"] = E(nc.sbuf_tensor("gs" + tag, [128, NTM], F32))
            fb["stt"] = E(nc.sbuf_tensor("stt" + tag, [128, 12], F32))
            fb["mv"] = E(nc.sbuf_tensor("mv" + tag, [128, 2 * NTM], F32))
            fb["rstd"] = E(nc.sbuf_tensor("rs" + tag, [128, NTM], F32))
            fb["xsl_free"] = [None, None]
            fb["ln_tm_free"] = [None, None]
            return fb

        with contextlib.ExitStack() as stack:
            fb = make_ffn_bufs(stack, "a")
            emit_ffn_pass(fb, x_in, x1)
            quiesce()
            P.flush()
            nc.all_engine_barrier()

        with contextlib.ExitStack() as stack:
            E = stack.enter_context
            ab = {}
            ab["K_fm"] = E(nc.sbuf_tensor("K_fm", [128, 8 * T], BF16))
            ab["v_tm"] = E(nc.sbuf_tensor("v_tm", [128, 32 * 1040], BF16))
            ab["xsl"] = [E(nc.sbuf_tensor("xat%d" % i, [128, 1024], F32)) for i in range(2)]
            ab["ln_tm"] = [E(nc.sbuf_tensor("ltat%d" % i, [128, 1024], BF16)) for i in range(2)]
            ab["stt"] = E(nc.sbuf_tensor("sttat", [128, 12], F32))
            ab["mv"] = E(nc.sbuf_tensor("mvat", [128, 8], F32))
            ab["rstd"] = E(nc.sbuf_tensor("rsat", [128, 4], F32))
            ab["xsl_free"] = [None, None]
            ab["ln_tm_free"] = [None, None]
            with contextlib.ExitStack() as s1:
                for i in range(8):
                    B[i] = Bank(s1.enter_context(nc.psum_tensor("pb1_%d" % i, [128, 512], F32)))
                ab["wkv_sb"] = s1.enter_context(nc.sbuf_tensor("wkv_sb", [128, 8 * 2048], BF16))
                ab["ln_fm"] = [s1.enter_context(nc.sbuf_tensor("lfa%d" % i, [128, 8 * 512], BF16)) for i in range(2)]
                emit_attn_b1(ab)
                quiesce()
                P.flush()
                nc.all_engine_barrier()
            with contextlib.ExitStack() as s2:
                E2c = s2.enter_context
                for i in range(8):
                    B[i] = Bank(E2c(nc.psum_tensor("pb2_%d" % i, [128, 512], F32)))
                ab["wq_sb"] = E2c(nc.sbuf_tensor("wq_sb", [128, 8 * 1024], BF16))
                ab["wo_sb"] = E2c(nc.sbuf_tensor("wo_sb", [128, 8 * 1024], BF16))
                ab["E2"] = [E2c(nc.sbuf_tensor("Eb%d" % i, [128, 16 * 128], BF16)) for i in range(4)]
                ab["q_sb"] = E2c(nc.sbuf_tensor("q_sb", [128, 8 * 128], BF16))
                ab["at_sb"] = E2c(nc.sbuf_tensor("at_sb", [128, 1024], BF16))
                ab["cc_sb"] = E2c(nc.sbuf_tensor("cc_sb", [128, 8 * 128], BF16))
                ab["ln_fmq"] = E2c(nc.sbuf_tensor("ln_fmq", [128, 8 * 128], BF16))
                ab["den"] = E2c(nc.sbuf_tensor("den", [128, 1], F32))
                emit_attn_b2(ab)
                quiesce()
                P.flush()
                nc.all_engine_barrier()

        with contextlib.ExitStack() as stack:
            fb = make_ffn_bufs(stack, "b")
            emit_ffn_pass(fb, x2, out)
            P.op("sync", lambda sy: sy.nop(), waits=[st2[0], st2[1]])
            quiesce()
            P.flush()
            nc.all_engine_barrier()

    return nc


_NC_CACHE = {}


def kernel(L0_attn_hidden_states, L0_attn_Wq, L0_attn_Wkv, L0_attn_Wo, L0_attn_bo,
           L0_ffn_input_ids, L0_ffn_gate_w, L0_ffn_gate_b, L0_ffn_w13, L0_ffn_w2,
           L0_ffn_w13_s, L0_ffn_w2_s):
    x = np.ascontiguousarray(np.asarray(L0_attn_hidden_states, dtype=np.float32))
    Wq = np.ascontiguousarray(np.asarray(L0_attn_Wq, dtype=np.float32))
    Wkv = np.ascontiguousarray(np.asarray(L0_attn_Wkv, dtype=np.float32))
    Wo = np.ascontiguousarray(np.asarray(L0_attn_Wo, dtype=np.float32))
    bo = np.asarray(L0_attn_bo, dtype=np.float32)
    gate_w = np.asarray(L0_ffn_gate_w, dtype=np.float32)
    gate_b = np.asarray(L0_ffn_gate_b, dtype=np.float32)
    w13 = np.asarray(L0_ffn_w13, dtype=np.float32)
    w2 = np.asarray(L0_ffn_w2, dtype=np.float32)
    w13_s = np.asarray(L0_ffn_w13_s, dtype=np.float32)
    w2_s = np.asarray(L0_ffn_w2_s, dtype=np.float32)

    w13t_ = np.ascontiguousarray(
        np.concatenate([w13.transpose(0, 2, 1), w13_s.T[None]], axis=0))
    w2t_ = np.ascontiguousarray(
        np.concatenate([w2.transpose(0, 2, 1), w2_s.T[None]], axis=0))
    gwt_ = np.ascontiguousarray(gate_w.T)
    gb_b_ = np.ascontiguousarray(
        np.broadcast_to(np.tile(gate_b, NTM)[None, :], (128, NTM * 16)))
    bo_b_ = np.ascontiguousarray(np.broadcast_to(bo[None, :], (128, D)))

    if "nc" not in _NC_CACHE:
        _NC_CACHE["nc"] = build_program()
    nc = _NC_CACHE["nc"]

    in_maps = []
    for b in range(N_CORES):
        in_maps.append({
            "x_in": np.ascontiguousarray(x[b]),
            "w13t": w13t_, "w2t": w2t_, "gwt": gwt_, "gb_b": gb_b_,
            "wq": Wq, "wkv": Wkv, "wo": Wo, "bo_b": bo_b_,
        })
    res = run_bass_kernel_spmd(nc, in_maps, core_ids=list(range(N_CORES)))
    return np.stack([r["out"] for r in res.results], axis=0)


# version 10 (final)
# speedup vs baseline: 1.1644x; 1.0231x over previous
import contextlib
import os
import numpy as np
import concourse.bass as bass
import concourse.mybir as mybir
from concourse.ap import AP
from concourse.bass_utils import run_bass_kernel_spmd

F32 = mybir.dt.float32
BF16 = mybir.dt.bfloat16
AF = mybir.ActivationFunctionType
ALU = mybir.AluOpType
AX = mybir.AxisListType

T = 4096
D = 1024
H = 16
NEX = 17
MC = 1024
NMC = T // MC
NTM = MC // 128
NS = MC // 512
EPS = 1e-5
N_CORES = 8


class Sem:
    def __init__(self, nc, name):
        self.h = nc.alloc_semaphore(name)
        self.v = 0


class Bank:

    def __init__(self, t):
        self.t = t
        self.free = None

    def w(self):
        return [self.free] if self.free else []


class Prog:
    ENGINES = ("tensor", "vector", "scalar", "gpsimd", "sync")

    def __init__(self, nc):
        self.nc = nc
        self.ops = {e: [] for e in self.ENGINES}
        self.hw = {e: {} for e in self.ENGINES}

    def op(self, eng, fn, waits=(), inc=None, incn=1):
        ws = []
        for w in waits:
            if w is None:
                continue
            if isinstance(w, tuple):
                s, v = w
            else:
                s, v = w, w.v
            if v <= 0:
                continue
            ws.append((s.h, v))
        hwe = self.hw[eng]
        ws2 = []
        for (h, v) in ws:
            key = h.num
            if hwe.get(key, -1) < v:
                hwe[key] = v
                ws2.append((h, v))
        if inc is not None:
            inc.v += incn

        def run(e, fn=fn, ws2=ws2, inc=inc, incn=incn):
            for (h, v) in ws2:
                e.wait_ge(h, v)
            inst = fn(e)
            if inc is not None:
                inst.then_inc(inc.h, incn)

        self.ops[eng].append(run)

    def flush(self):
        with self.nc.Block() as blk:
            for name in self.ENGINES:
                if self.ops[name]:
                    getattr(blk, name)(
                        lambda eng, _n=name: [f(eng) for f in self.ops[_n]])
        self.ops = {e: [] for e in self.ENGINES}
        self.hw = {e: {} for e in self.ENGINES}


def build_program():
    nc = bass.Bass("TRN2", target_bir_lowering=False)

    x_in = nc.dram_tensor("x_in", [T, D], F32, kind="ExternalInput").ap()
    w13t = nc.dram_tensor("w13t", [NEX, D, D], F32, kind="ExternalInput").ap()
    w2t = nc.dram_tensor("w2t", [NEX, 512, D], F32, kind="ExternalInput").ap()
    gwt = nc.dram_tensor("gwt", [D, 16], F32, kind="ExternalInput").ap()
    gb_b = nc.dram_tensor("gb_b", [128, NTM * 16], F32, kind="ExternalInput").ap()
    wq = nc.dram_tensor("wq", [D, D], F32, kind="ExternalInput").ap()
    wkv = nc.dram_tensor("wkv", [D, 2 * D], F32, kind="ExternalInput").ap()
    wo = nc.dram_tensor("wo", [D, D], F32, kind="ExternalInput").ap()
    bo_b = nc.dram_tensor("bo_b", [128, D], F32, kind="ExternalInput").ap()
    out = nc.dram_tensor("out", [T, D], F32, kind="ExternalOutput").ap()
    _tap = "ExternalOutput" if os.environ.get("KMK_DEBUG_TAPS") else "Internal"
    x1 = nc.dram_tensor("x1s", [T, D], F32, kind=_tap).ap()
    x2 = nc.dram_tensor("x2s", [T, D], F32, kind=_tap).ap()
    k_scr = nc.dram_tensor("k_scr", [8, 128, T], BF16, kind="Internal").ap()
    v_scr = nc.dram_tensor("v_scr", [8, 128, 32 * 130], BF16, kind="Internal").ap()

    with nc.cleanup_on_exit():
        ident_bf = nc.alloc_sbuf_tensor("ident_bf", [128, 128], BF16)
        ident_f = nc.alloc_sbuf_tensor("ident_f", [128, 128], F32)
        consts = nc.alloc_sbuf_tensor("consts", [128, 4], F32)
        gbb_sb = nc.alloc_sbuf_tensor("gbb_sb", [128, NTM * 16], F32)
        bob_sb = nc.alloc_sbuf_tensor("bob_sb", [128, D], F32)
        gwt_sb = nc.alloc_sbuf_tensor("gwt_sb", [128, 8 * 16], BF16)
        B = [None] * 8

        mm = Sem(nc, "mm"); ev = Sem(nc, "ev"); es = Sem(nc, "es")
        gv = Sem(nc, "gv"); ldi = Sem(nc, "ldi")
        ldw2 = [Sem(nc, "ldwa"), Sem(nc, "ldwb")]
        lx2 = [Sem(nc, "lxa"), Sem(nc, "lxb")]
        st2 = [Sem(nc, "sta"), Sem(nc, "stb")]
        blx = [Sem(nc, "blxa"), Sem(nc, "blxb")]
        bst = Sem(nc, "bst")

        P = Prog(nc)

        P.op("gpsimd", lambda g: g.memset(ident_bf[:], 0.0), waits=[gv], inc=gv)
        P.op("gpsimd", lambda g: g.affine_select(
            out=ident_bf[:], in_=ident_bf[:], compare_op=ALU.not_equal,
            fill=1.0, base=0, pattern=[[-1, 128]], channel_multiplier=1),
            waits=[gv], inc=gv)
        P.op("gpsimd", lambda g: g.memset(ident_f[:], 0.0), waits=[gv], inc=gv)
        P.op("gpsimd", lambda g: g.affine_select(
            out=ident_f[:], in_=ident_f[:], compare_op=ALU.not_equal,
            fill=1.0, base=0, pattern=[[-1, 128]], channel_multiplier=1),
            waits=[gv], inc=gv)
        P.op("gpsimd", lambda g: g.memset(consts[:, 0:1], EPS), waits=[gv], inc=gv)
        P.op("gpsimd", lambda g: g.memset(consts[:, 1:2], -0.5), waits=[gv], inc=gv)
        P.op("gpsimd", lambda g: g.memset(consts[:, 2:3], 0.125), waits=[gv], inc=gv)
        P.op("gpsimd", lambda g: g.dma_start(gbb_sb[:], gb_b), inc=ldi, incn=16)
        P.op("gpsimd", lambda g: g.dma_start(bob_sb[:], bo_b), inc=ldi, incn=16)
        P.op("gpsimd", lambda g: g.dma_start(
            gwt_sb[:], gwt.rearrange("(kc p) e -> p kc e", p=128)), inc=ldi, incn=16)
        init_snap = (ldi, ldi.v)
        gv_snap = (gv, gv.v)

        def quiesce(extra=()):
            P.op("gpsimd", lambda g: g.nop(),
                 waits=[ldw2[0], ldw2[1], lx2[0], lx2[1], st2[0], st2[1], ldi]
                 + list(extra))

        def emit_ln(src, row0, ntiles, xsl, xsl_free, lxs, ln_tm, ln_tm_free,
                    stt, mv, rstd, ln_fm_sb, fm_cols, tp_banks, dst_war=None,
                    keep_x=None, src_ready=None):
            for ti in range(ntiles):
                slot = xsl[ti % len(xsl)]
                lsem = lxs[ti % len(xsl)]
                srw = src_ready[row0 // 128 + ti] if src_ready else None
                P.op("sync", lambda sy, s=slot, r=row0 + ti * 128:
                     sy.dma_start(s[:], src[r:r + 128, :]),
                     waits=[xsl_free[ti % len(xsl)], srw], inc=lsem, incn=16)
                ld_snap = (lsem, lsem.v)
                P.op("vector", lambda v, s=slot: v.bn_stats(stt[:, 0:6], s[:, 0:512]),
                     waits=[ld_snap, ev])
                P.op("vector", lambda v, s=slot: v.bn_stats(stt[:, 6:12], s[:, 512:1024]),
                     inc=ev)
                P.op("vector", lambda v, ti=ti: v.bn_aggr(mv[:, 2 * ti:2 * ti + 2], stt[:]),
                     waits=[ev], inc=ev)
                P.op("scalar", lambda s, ti=ti: s.activation(
                    rstd[:, ti:ti + 1], mv[:, 2 * ti + 1:2 * ti + 2], AF.Ln,
                    bias=consts[:, 0:1]), waits=[ev, es, gv_snap], inc=es)
                P.op("scalar", lambda s, ti=ti: s.activation(
                    rstd[:, ti:ti + 1], rstd[:, ti:ti + 1], AF.Exp,
                    scale=consts[:, 1:2]), waits=[es], inc=es)
                tslot = ln_tm[ti % len(ln_tm)]
                P.op("vector", lambda v, s=slot, t=tslot, ti=ti: v.tensor_scalar(
                    t[:], s[:], mv[:, 2 * ti:2 * ti + 1], rstd[:, ti:ti + 1],
                    ALU.subtract, ALU.mult),
                    waits=[es, ev, ln_tm_free[ti % len(ln_tm)]], inc=ev)
                apply_snap = (ev, ev.v)
                if keep_x is not None:
                    keep_x[ti] = slot
                else:
                    xsl_free[ti % len(xsl)] = apply_snap
                for bk in range(2):
                    bank = tp_banks[bk]
                    P.op("tensor", lambda t, tt=tslot, b=bank.t, bk=bk: [
                        t.matmul(b[:, j * 128:(j + 1) * 128],
                                 tt[:, (bk * 4 + j) * 128:(bk * 4 + j + 1) * 128],
                                 ident_bf[:], start=True, stop=True)
                        for j in range(4)][-1],
                        waits=[apply_snap, gv_snap] + bank.w(), inc=mm)
                    mmsnap = (mm, mm.v)
                    P.op("vector", lambda v, b=bank.t, ti=ti, bk=bk: v.tensor_copy(
                        AP(tensor=ln_fm_sb.ap().tensor,
                           offset=(bk * 4) * fm_cols + ti * 128,
                           ap=[ln_fm_sb.ap().ap[0], [fm_cols, 4], [1, 128]]),
                        b[:, 0:512]),
                        waits=[mmsnap, dst_war], inc=ev)
                    bank.free = (ev, ev.v)
                ln_tm_free[ti % len(ln_tm)] = (mm, mm.v)
            return (ev, ev.v)

        def emit_ffn_pass(fb, src, dst, hook=None, dst_done=None,
                          src_ready=None):
            ln_fm = fb["ln_fm"]; y_acc = fb["y_acc"]
            w13b = fb["w13b"]; w2b = fb["w2b"]
            xsl = fb["xsl"]; ln_tm = fb["ln_tm"]
            h_sb = fb["h_sb"]; sg = fb["sg"]
            sc_tm = fb["sc_tm"]; bt = fb["bt"]; msk = fb["msk"]; gsum = fb["gsum"]
            stt = fb["stt"]; mv = fb["mv"]; rstd = fb["rstd"]

            w13_free = [None, None]; w2_free = [None, None]
            xsl_free = fb["xsl_free"]
            ln_tm_free = fb["ln_tm_free"]
            sg_free = [None, None]
            h_war = {}
            yacc_free = None
            lnfm_war = None

            def preload(idx):
                e = idx % NEX
                buf = idx % 2
                lsem = ldw2[buf]
                P.op("gpsimd", lambda g, e=e, b=buf: g.dma_start(
                    w13b[b][:], w13t[e].rearrange("(kc p) j -> p kc j", p=128)),
                    waits=[w13_free[buf]], inc=lsem, incn=16)
                P.op("gpsimd", lambda g, e=e, b=buf: g.dma_start(
                    w2b[b][:], w2t[e].rearrange("(ic p) d -> p ic d", p=128)),
                    waits=[w2_free[buf]], inc=lsem, incn=16)
                return (lsem, lsem.v)

            w_snaps = {0: preload(0)}
            sidx = 0
            gu_banks = fb.get("gu_banks") or [B[0], B[1], B[2]]
            y_banks = fb.get("y_banks") or [B[3], B[4], B[5], B[6], B[7]]
            nyb = len(y_banks)
            gu_i = [0]; y_i = [0]

            for mc in range(NMC):
                lnfm_ready = emit_ln(src, mc * MC, NTM, xsl, xsl_free, lx2, ln_tm,
                                     ln_tm_free, stt, mv, rstd, ln_fm, MC,
                                     [gu_banks[0], gu_banks[1]], dst_war=lnfm_war,
                                     src_ready=src_ready)
                gate_bank = gu_banks[-1]
                for half in range(NTM // 8):
                    P.op("tensor", lambda t, half=half, b=gate_bank.t: [
                        t.matmul(b[:, tj * 16:tj * 16 + 16],
                                 ln_fm[:, kc * MC + (half * 8 + tj) * 128:
                                       kc * MC + (half * 8 + tj) * 128 + 128],
                                 gwt_sb[:, kc * 16:kc * 16 + 16],
                                 start=(kc == 0), stop=(kc == 7))
                        for tj in range(8) for kc in range(8)][-1],
                        waits=[lnfm_ready, init_snap] + gate_bank.w(), inc=mm)
                    mmsnap = (mm, mm.v)
                    P.op("scalar", lambda s, half=half, b=gate_bank.t: s.activation(
                        sc_tm[:, half * 128:half * 128 + 128], b[:, 0:128],
                        AF.Sigmoid), waits=[mmsnap], inc=es)
                    gate_bank.free = (es, es.v)

                GE = 16
                P.op("vector", lambda v: v.tensor_add(bt[0][:], sc_tm[:], gbb_sb[:]),
                     waits=[es, ev], inc=ev)

                def lanes_ap(tens, fixed):
                    base = tens.ap()
                    off = sum(b_ for b_, val in fixed.items() if val)
                    dims = [[GE, NTM]]
                    for b_ in (8, 4, 2, 1):
                        if b_ not in fixed:
                            dims.append([b_, 2])
                    merged = [dims[0]]
                    for d_ in dims[1:]:
                        o = merged[-1]
                        if o[0] == d_[0] * d_[1]:
                            merged[-1] = [d_[0], d_[1] * o[1]]
                        else:
                            merged.append(d_)
                    return AP(tensor=base.tensor, offset=off,
                              ap=[base.ap[0]] + merged)

                srcb, dstb = bt[0], bt[1]
                for k in (2, 4, 8, 16):
                    j = k // 2
                    while j >= 1:
                        pairs = [({j: 0}, {j: 1}, ALU.max, ALU.min)] if k == 16 else [
                            ({j: 0, k: 0}, {j: 1, k: 0}, ALU.max, ALU.min),
                            ({j: 0, k: 1}, {j: 1, k: 1}, ALU.min, ALU.max)]
                        first = True
                        for (lo, hi, o1, o2) in pairs:
                            P.op("vector", lambda v, s=srcb, d2=dstb, lo=lo, hi=hi, o1=o1:
                                 v.tensor_tensor(lanes_ap(d2, lo), lanes_ap(s, lo),
                                                 lanes_ap(s, hi), o1),
                                 waits=([ev] if first else []))
                            first = False
                            P.op("vector", lambda v, s=srcb, d2=dstb, lo=lo, hi=hi, o2=o2:
                                 v.tensor_tensor(lanes_ap(d2, hi), lanes_ap(s, lo),
                                                 lanes_ap(s, hi), o2),
                                 inc=ev)
                        srcb, dstb = dstb, srcb
                        j //= 2
                P.op("vector", lambda v, d2=dstb: v.tensor_add(d2[:], sc_tm[:], gbb_sb[:]),
                     waits=[ev], inc=ev)
                theta = AP(tensor=srcb.ap().tensor, offset=3,
                           ap=[srcb.ap().ap[0], [GE, NTM], [0, GE]])

                def g3(t_):
                    return AP(tensor=t_.ap().tensor, offset=0,
                              ap=[t_.ap().ap[0], [GE, NTM], [1, GE]])

                P.op("vector", lambda v, d2=dstb: v.tensor_tensor(
                    g3(msk), g3(d2), theta, ALU.is_ge), waits=[ev], inc=ev)
                P.op("vector", lambda v: v.tensor_mul(msk[:], sc_tm[:], msk[:]),
                     waits=[ev], inc=ev)
                P.op("vector", lambda v: v.reduce_sum(gsum[:], g3(msk), AX.X),
                     waits=[ev], inc=ev)
                P.op("vector", lambda v: v.reciprocal(gsum[:], gsum[:]), waits=[ev], inc=ev)
                P.op("vector", lambda v: v.tensor_scalar(
                    gsum[:], gsum[:], 2.5, None, ALU.mult), waits=[ev], inc=ev)
                rb = AP(tensor=gsum.ap().tensor, offset=0,
                        ap=[gsum.ap().ap[0], [1, NTM], [0, GE]])
                P.op("vector", lambda v: v.tensor_tensor(g3(msk), g3(msk), rb, ALU.mult),
                     waits=[ev], inc=ev)
                msk_ready = (ev, ev.v)

                for e in range(NEX):
                    if hook is not None:
                        hook(mc, e)
                    buf = sidx % 2
                    w_snap = w_snaps[sidx]
                    w_snaps[sidx + 1] = preload(sidx + 1)
                    is_shared = (e == NEX - 1)

                    h_readys = {}

                    def emit_y(s):
                            for tsub in range(4):
                                cc = (s * 4 + tsub) * 16 + e
                                for dhf in range(2):
                                    yi = y_i[0]; y_i[0] += 1
                                    bank = y_banks[yi % nyb]
                                    P.op("tensor", lambda t, s=s, tsub=tsub, dhf=dhf,
                                         b=buf, bk=bank.t: [
                                             t.matmul(bk[:, 0:512],
                                                      h_sb[:, ic * MC + s * 512 + tsub * 128:
                                                            ic * MC + s * 512 + tsub * 128 + 128],
                                                      w2b[b][:, ic * 1024 + dhf * 512:
                                                            ic * 1024 + dhf * 512 + 512],
                                                      start=(ic == 0), stop=(ic == 3))
                                             for ic in range(4)][-1],
                                        waits=[h_readys[s], w_snap] + bank.w(), inc=mm)
                                    mmsnap = (mm, mm.v)
                                    ycol = (s * 4 + tsub) * 1024 + dhf * 512
                                    if is_shared:
                                        P.op("vector", lambda v, ycol=ycol, bk=bank.t:
                                             v.tensor_tensor(y_acc[:, ycol:ycol + 512],
                                                             bk[:, 0:512],
                                                             y_acc[:, ycol:ycol + 512],
                                                             ALU.add),
                                             waits=[mmsnap, ev], inc=ev)
                                    elif e == 0:
                                        P.op("vector", lambda v, ycol=ycol, bk=bank.t,
                                             cc=cc: v.tensor_scalar(
                                                 y_acc[:, ycol:ycol + 512], bk[:, 0:512],
                                                 msk[:, cc:cc + 1], None, ALU.mult),
                                             waits=[mmsnap, msk_ready, yacc_free], inc=ev)
                                    else:
                                        P.op("vector", lambda v, ycol=ycol, bk=bank.t,
                                             cc=cc: v.scalar_tensor_tensor(
                                                 y_acc[:, ycol:ycol + 512], bk[:, 0:512],
                                                 msk[:, cc:cc + 1],
                                                 y_acc[:, ycol:ycol + 512],
                                                 ALU.mult, ALU.add),
                                             waits=[mmsnap, ev], inc=ev)
                                    bank.free = (ev, ev.v)
                            h_war[s] = (mm, mm.v)

                    for s in range(NS):
                        sg_snap = None
                        for pair in range(4):
                            sslot = pair % 2
                            for gate_part in (True, False):
                                jg = pair if gate_part else pair + 4
                                gi = gu_i[0]; gu_i[0] += 1
                                bank = gu_banks[gi % len(gu_banks)]
                                P.op("tensor", lambda t, jg=jg, s=s, b=buf, bk=bank.t: [
                                    t.matmul(bk[:, 0:512],
                                             w13b[b][:, kc * 1024 + jg * 128:
                                                   kc * 1024 + jg * 128 + 128],
                                             ln_fm[:, kc * MC + s * 512:
                                                   kc * MC + s * 512 + 512],
                                             start=(kc == 0), stop=(kc == 7))
                                    for kc in range(8)][-1],
                                    waits=[w_snap, lnfm_ready] + bank.w(), inc=mm)
                                mmsnap = (mm, mm.v)
                                if gate_part:
                                    P.op("scalar", lambda sc, sslot=sslot, bk=bank.t:
                                         sc.activation(sg[sslot][:], bk[:, 0:512],
                                                       AF.Silu),
                                         waits=[mmsnap, sg_free[sslot]], inc=es)
                                    sg_snap = (es, es.v)
                                    bank.free = (es, es.v)
                                else:
                                    ic = pair
                                    hcol = ic * MC + s * 512
                                    hw_ = h_war.get(s)
                                    P.op("vector", lambda v, hcol=hcol, bk=bank.t,
                                         sslot=sslot: v.tensor_tensor(
                                             h_sb[:, hcol:hcol + 512],
                                             bk[:, 0:512], sg[sslot][:],
                                             ALU.mult),
                                         waits=[mmsnap, sg_snap, hw_], inc=ev)
                                    bank.free = (ev, ev.v)
                                    sg_free[sslot] = (ev, ev.v)
                        h_readys[s] = (ev, ev.v)
                        if s >= 1:
                            emit_y(s - 1)
                    emit_y(NS - 1)
                    w13_free[buf] = (mm, mm.v)
                    w2_free[buf] = (mm, mm.v)
                    sidx += 1
                lnfm_war = (mm, mm.v)
                y_done = (ev, ev.v)

                for ti in range(NTM):
                    slot = xsl[ti % 2]
                    lsem = lx2[ti % 2]
                    ssem = st2[ti % 2]
                    srw = src_ready[mc * NTM + ti] if src_ready else None
                    P.op("sync", lambda sy, s=slot, r=mc * MC + ti * 128:
                         sy.dma_start(s[:], src[r:r + 128, :]),
                         waits=[xsl_free[ti % 2], srw], inc=lsem, incn=16)
                    ld_snap = (lsem, lsem.v)
                    P.op("vector", lambda v, s=slot, ti=ti:
                         v.scalar_tensor_tensor(
                             s[:], y_acc[:, ti * 1024:ti * 1024 + 1024], 0.5, s[:],
                             ALU.mult, ALU.add),
                         waits=[ld_snap, y_done], inc=ev)
                    evsnap = (ev, ev.v)
                    P.op("sync", lambda sy, s=slot, r=mc * MC + ti * 128:
                         sy.dma_start(dst[r:r + 128, :], s[:]),
                         waits=[evsnap], inc=ssem, incn=16)
                    xsl_free[ti % 2] = (ssem, ssem.v)
                    if dst_done is not None:
                        dst_done[mc * NTM + ti] = (ssem, ssem.v)
                yacc_free = (ev, ev.v)

        def make_b1_bufs(E):
            bb = {}
            bb["wkv_sb"] = E(nc.sbuf_tensor("wkv_sb", [128, 8 * 2048], BF16))
            bb["ln_fm2"] = E(nc.sbuf_tensor("blnfm", [128, 8 * 512], BF16))
            bb["xsl"] = [E(nc.sbuf_tensor("bx%d" % i, [128, 1024], F32)) for i in range(2)]
            bb["ln_tm"] = [E(nc.sbuf_tensor("blt%d" % i, [128, 1024], BF16)) for i in range(2)]
            bb["stt"] = E(nc.sbuf_tensor("bstt", [128, 12], F32))
            bb["mv"] = E(nc.sbuf_tensor("bmv", [128, 8], F32))
            bb["rstd"] = E(nc.sbuf_tensor("brs", [128, 4], F32))
            bb["kstg"] = [E(nc.sbuf_tensor("kst%d" % i, [128, 512], BF16)) for i in range(2)]
            bb["vstg"] = [E(nc.sbuf_tensor("vst%d" % i, [128, 1040], BF16)) for i in range(2)]
            bb["xsl_free"] = [None, None]; bb["ln_tm_free"] = [None, None]
            bb["kstg_free"] = [None, None]; bb["vstg_free"] = [None, None]
            bb["lnfm_war"] = [None]
            P.op("gpsimd", lambda g: g.dma_start(
                bb["wkv_sb"][:], wkv.rearrange("(kc p) n -> p kc n", p=128)),
                inc=ldi, incn=16)
            bb["wkv_snap"] = (ldi, ldi.v)
            for i in range(2):
                P.op("gpsimd", lambda g, i=i: g.memset(bb["vstg"][i][:], 1.0),
                     waits=[gv], inc=gv)
            bb["ones_snap"] = (gv, gv.v)
            return bb

        def emit_b1_chunk(bb, c):
            wkv_sb = bb["wkv_sb"]; ln_fm2 = bb["ln_fm2"]
            kstg = bb["kstg"]; vstg = bb["vstg"]
            wkv_snap = bb["wkv_snap"]
            kvb = bb["kvb"]
            ln_ready = emit_ln(x1, c * 512, 4, bb["xsl"], bb["xsl_free"], blx,
                               bb["ln_tm"], bb["ln_tm_free"], bb["stt"], bb["mv"],
                               bb["rstd"], ln_fm2, 512, kvb,
                               dst_war=bb["lnfm_war"][0],
                               src_ready=bb["x1_done"])
            for ic in range(8):
                bank = kvb[ic % 2]
                par = ic % 2
                P.op("tensor", lambda t, ic=ic, bk=bank.t: [
                    t.matmul(bk[:, 0:512],
                             wkv_sb[:, kc * 2048 + ic * 128:kc * 2048 + ic * 128 + 128],
                             ln_fm2[:, kc * 512:kc * 512 + 512],
                             start=(kc == 0), stop=(kc == 7))
                    for kc in range(8)][-1],
                    waits=[ln_ready, wkv_snap] + bank.w(), inc=mm)
                mmsnap = (mm, mm.v)
                P.op("scalar", lambda s, par=par, bk=bank.t: s.copy(
                    kstg[par][:], bk[:, 0:512]),
                    waits=[mmsnap, bb["kstg_free"][par]], inc=es)
                essnap = (es, es.v)
                P.op("scalar", lambda s, ic=ic, c=c, par=par: s.dma_start(
                    k_scr[ic][:, c * 512:c * 512 + 512], kstg[par][:]),
                    waits=[essnap], inc=bst, incn=16)
                bb["kstg_free"][par] = (bst, bst.v)
                bank.free = (es, es.v)
            for tsub in range(4):
                ch = c * 4 + tsub
                par = tsub % 2
                for vh in range(2):
                    bank = kvb[vh]
                    P.op("tensor", lambda t, tsub=tsub, vh=vh, bk=bank.t: [
                        t.matmul(bk[:, 0:512],
                                 ln_fm2[:, kc * 512 + tsub * 128:
                                        kc * 512 + tsub * 128 + 128],
                                 wkv_sb[:, kc * 2048 + 1024 + vh * 512:
                                        kc * 2048 + 1024 + vh * 512 + 512],
                                 start=(kc == 0), stop=(kc == 7))
                        for kc in range(8)][-1],
                        waits=[ln_ready, wkv_snap] + bank.w(), inc=mm)
                    mmsnap = (mm, mm.v)
                    row = vh * 8 * 65
                    P.op("vector", lambda v, row=row, par=par, bk=bank.t:
                         v.tensor_copy(
                             AP(tensor=vstg[par].ap().tensor, offset=row,
                                ap=[vstg[par].ap().ap[0], [65, 8], [1, 64]]),
                             bk[:, 0:512]),
                         waits=[mmsnap, bb["vstg_free"][par],
                                bb["ones_snap"]], inc=ev)
                    bank.free = (ev, ev.v)
                evsnap = (ev, ev.v)
                for hp in range(8):
                    P.op("scalar", lambda s, hp=hp, ch=ch, par=par: s.dma_start(
                        v_scr[hp][:, ch * 130:(ch + 1) * 130],
                        vstg[par][:, hp * 130:(hp + 1) * 130]),
                        waits=[evsnap], inc=bst, incn=16)
                bb["vstg_free"][par] = (bst, bst.v)
            bb["lnfm_war"][0] = (mm, mm.v)

        def emit_attn_b2(ab):
            K_fm = ab["K_fm"]; v_tm = ab["v_tm"]
            wq_sb = ab["wq_sb"]; wo_sb = ab["wo_sb"]
            E2 = ab["E2"]; q_sb = ab["q_sb"]; at_sb = ab["at_sb"]; cc_sb = ab["cc_sb"]
            xsl = ab["xsl"]; ln_tm = ab["ln_tm"]; den = ab["den"]
            stt = ab["stt"]; mv = ab["mv"]; rstd = ab["rstd"]
            ln_fmq = ab["ln_fmq"]
            xsl_free = ab["xsl_free"]; ln_tm_free = ab["ln_tm_free"]
            kv_snap = ab.get("kv_snap")

            P.op("gpsimd", lambda g: g.dma_start(
                wq_sb[:], wq.rearrange("(kc p) n -> p kc n", p=128)), inc=ldi, incn=16)
            P.op("gpsimd", lambda g: g.dma_start(
                wo_sb[:], wo.rearrange("(kc p) n -> p kc n", p=128)), inc=ldi, incn=16)
            w_snap = (ldi, ldi.v)

            E_free = [None, None, None, None]
            proj = [B[2], B[3]]
            scb = [B[4], B[5]]
            scb2 = [B[6], B[7]]
            pvb = B[0]
            pvb2 = B[1]
            atb = [at_sb, ln_tm[1]]
            st_ctx = {"at_war": [None, None], "cc_war": None, "lnfmq_war": None,
                      "at": at_sb, "atq": 0}

            def emit_pv_half(h, pvbk, exp_sn, ebufs, half):
                eb = E2[ebufs[half]]
                for gg in range(4):
                    g = half * 4 + gg
                    P.op("tensor", lambda t, g=g, gg=gg, h=h, eb=eb, bk=pvbk.t: [
                        t.matmul(bk[:, 0:65],
                                 eb[:, (gg * 4 + cj) * 128:(gg * 4 + cj) * 128 + 128],
                                 v_tm[:, (g * 4 + cj) * 1040 + h * 65:
                                       (g * 4 + cj) * 1040 + h * 65 + 65],
                                 start=(g == 0 and cj == 0),
                                 stop=(g == 7 and cj == 3))
                        for cj in range(4)][-1],
                        waits=[exp_sn[g], kv_snap] + (pvbk.w() if g == 0 else []),
                        inc=mm)
                E_free[ebufs[half]] = (mm, mm.v)
                if half == 1:
                    pv_done = (mm, mm.v)
                    P.op("vector", lambda v, bk=pvbk.t: v.reciprocal(
                        den[:, 0:1], bk[:, 64:65]), waits=[pv_done, ev], inc=ev)
                    P.op("vector", lambda v, h=h, a=st_ctx["at"], bk=pvbk.t:
                         v.tensor_scalar(
                             a[:, h * 64:h * 64 + 64], bk[:, 0:64], den[:, 0:1],
                             None, ALU.mult),
                         waits=[ev] + ([st_ctx["at_war"][st_ctx["atq"] % 2]]
                                       if h == 0 else []), inc=ev)
                    pvbk.free = (ev, ev.v)

            def emit_qk_g(g, ic, exp_snaps, h0, h1, q_ready):
                bank0 = scb[g % 2]
                bank1 = scb2[g % 2]
                P.op("tensor", lambda t, g=g, ic=ic, b0=bank0.t, b1=bank1.t: [
                    t.matmul((b0 if hh == 0 else b1)[:, cj * 128:cj * 128 + 128],
                             K_fm.ap()[hh * 64:hh * 64 + 64,
                                       ic * T + (g * 4 + cj) * 128:
                                       ic * T + (g * 4 + cj) * 128 + 128],
                             q_sb.ap()[hh * 64:hh * 64 + 64,
                                       ic * 128:ic * 128 + 128],
                             start=True, stop=True, tile_position=(hh * 64, 0))
                    for cj in range(4) for hh in (0, 1)][-1],
                    waits=[q_ready, kv_snap] + bank0.w() + bank1.w(), inc=mm)
                mmsnap = (mm, mm.v)
                ecol = (g % 4) * 512
                for (h, bank, ebi) in ((h0, bank0, 0 if g < 4 else 1),
                                       (h1, bank1, 2 if g < 4 else 3)):
                    P.op("scalar", lambda s, ecol=ecol, eb=E2[ebi], bk=bank.t:
                         s.activation(eb[:, ecol:ecol + 512], bk[:, 0:512], AF.Exp),
                         waits=[mmsnap, E_free[ebi]], inc=es)
                    exp_snaps[h].append((es, es.v))
                    bank.free = (es, es.v)

            def prep_q(q):
                keep = {}
                sub_free = [xsl_free[q % 2]]
                ln_ready = emit_ln(x1, q * 128, 1, [xsl[q % 2]], sub_free,
                                   [lx2[q % 2]], ln_tm, ln_tm_free,
                                   stt, mv, rstd, ln_fmq, 128, [proj[0], proj[1]],
                                   dst_war=st_ctx["lnfmq_war"], keep_x=keep)
                for bk2 in range(2):
                    bank = proj[bk2]
                    P.op("tensor", lambda t, bk2=bk2, bk=bank.t: [
                        t.matmul(bk[:, gj * 128:gj * 128 + 128],
                                 wq_sb[:, kc * 1024 + (bk2 * 4 + gj) * 128:
                                       kc * 1024 + (bk2 * 4 + gj) * 128 + 128],
                                 ln_fmq[:, kc * 128:kc * 128 + 128],
                                 start=(kc == 0), stop=(kc == 7))
                        for gj in range(4) for kc in range(8)][-1],
                        waits=[ln_ready, w_snap] + bank.w(), inc=mm)
                st_ctx["lnfmq_war"] = (mm, mm.v)
                return {"xq": keep[0], "qmm": (mm, mm.v)}

            def emit_qevac(qs):
                for bk2 in range(2):
                    bank = proj[bk2]
                    P.op("scalar", lambda sc, bk2=bk2, bk=bank.t: sc.activation(
                        q_sb[:, bk2 * 512:bk2 * 512 + 512], bk[:, 0:512], AF.Copy,
                        scale=consts[:, 2:3]), waits=[qs["qmm"], gv_snap, es], inc=es)
                    bank.free = (es, es.v)
                return (es, es.v)

            def emit_tail_a(q):
                a = atb[q % 2]
                at_ready = (ev, ev.v)
                for rnd in range(4):
                    bank = (scb + scb2)[rnd]
                    P.op("tensor", lambda t, rnd=rnd, a=a, bk=bank.t: [
                        t.matmul(bk.ap()[0:64, j * 128:j * 128 + 128],
                                 a[:, (rnd * 4 + j) * 64:(rnd * 4 + j + 1) * 64],
                                 ident_bf[:], start=True, stop=True)
                        for j in range(4)][-1],
                        waits=[at_ready, gv_snap] + bank.w(), inc=mm)
                    mmsnap = (mm, mm.v)
                    for j in range(4):
                        hh = rnd * 4 + j
                        P.op("vector", lambda v, hh=hh, j=j, bk=bank.t: v.tensor_copy(
                            cc_sb.ap()[(hh % 2) * 64:(hh % 2) * 64 + 64,
                                       (hh // 2) * 128:(hh // 2) * 128 + 128],
                            bk.ap()[0:64, j * 128:j * 128 + 128]),
                            waits=[mmsnap] + ([st_ctx["cc_war"]] if (rnd == 0 and j == 0) else []),
                            inc=ev)
                    bank.free = (ev, ev.v)
                st_ctx["at_war"][q % 2] = (mm, mm.v)
                return (ev, ev.v)

            def emit_tail_b(q, qs, cc_ready):
                xq = qs["xq"]
                P.op("vector", lambda v, xq=xq: v.tensor_tensor(
                    xq[:], xq[:], bob_sb[:], ALU.add), waits=[ev], inc=ev)
                for dhf in range(2):
                    bank = proj[dhf]
                    P.op("tensor", lambda t, dhf=dhf, bk=bank.t: [
                        t.matmul(bk[:, 0:512],
                                 cc_sb[:, ic2 * 128:ic2 * 128 + 128],
                                 wo_sb[:, ic2 * 1024 + dhf * 512:
                                       ic2 * 1024 + dhf * 512 + 512],
                                 start=(ic2 == 0), stop=(ic2 == 7))
                        for ic2 in range(8)][-1],
                        waits=[cc_ready, w_snap] + bank.w(), inc=mm)
                    mmsnap = (mm, mm.v)
                    P.op("vector", lambda v, dhf=dhf, xq=xq, bk=bank.t:
                         v.tensor_tensor(xq[:, dhf * 512:dhf * 512 + 512],
                                         bk[:, 0:512],
                                         xq[:, dhf * 512:dhf * 512 + 512],
                                         ALU.add),
                         waits=[mmsnap, ev], inc=ev)
                    bank.free = (ev, ev.v)
                st_ctx["cc_war"] = (mm, mm.v)
                evsnap = (ev, ev.v)
                ssem = st2[q % 2]
                P.op("sync", lambda sy, xq=xq, q=q, ssem=ssem: sy.dma_start(
                    x2[q * 128:(q + 1) * 128, :], xq[:]), waits=[evsnap],
                    inc=ssem, incn=16)
                xsl_free[q % 2] = (ssem, ssem.v)

            qs_cur = prep_q(0)
            q_ready = emit_qevac(qs_cur)
            qs_next = None
            tail_q = None
            tail_qs = None
            cc_ready = None
            for q in range(32):
                st_ctx["at"] = atb[q % 2]
                st_ctx["atq"] = q
                pend = {}
                for hp in range(H // 2):
                    h0, h1 = 2 * hp, 2 * hp + 1
                    ic = hp
                    exp_snaps = {h0: [], h1: []}
                    if "h1A" in pend:
                        emit_pv_half(*pend.pop("h1A"), 0)
                    for g in range(4):
                        emit_qk_g(g, ic, exp_snaps, h0, h1, q_ready)
                    if "h0B" in pend:
                        emit_pv_half(*pend.pop("h0B"), 1)
                    if "h1B" in pend:
                        emit_pv_half(*pend.pop("h1B"), 1)
                    for g in range(4, 8):
                        emit_qk_g(g, ic, exp_snaps, h0, h1, q_ready)
                    if hp == 2 and tail_q is not None:
                        cc_ready = emit_tail_a(tail_q)
                    if hp == 3 and tail_q is not None:
                        emit_tail_b(tail_q, tail_qs, cc_ready)
                        tail_q = None
                    if hp == 4 and q + 1 < 32:
                        qs_next = prep_q(q + 1)
                    emit_pv_half(h0, pvb, exp_snaps[h0], (0, 1), 0)
                    pend["h0B"] = (h0, pvb, exp_snaps[h0], (0, 1))
                    pend["h1A"] = (h1, pvb2, exp_snaps[h1], (2, 3))
                    pend["h1B"] = (h1, pvb2, exp_snaps[h1], (2, 3))
                emit_pv_half(*pend.pop("h1A"), 0)
                emit_pv_half(*pend.pop("h0B"), 1)
                emit_pv_half(*pend.pop("h1B"), 1)
                tail_q = q
                tail_qs = qs_cur
                if q + 1 < 32:
                    q_ready = emit_qevac(qs_next)
                    qs_cur = qs_next
            cc_ready = emit_tail_a(tail_q)
            emit_tail_b(tail_q, tail_qs, cc_ready)

        def make_ffn_bufs(stack, tag):
            fb = {}
            E = stack.enter_context
            for i in range(8):
                B[i] = Bank(E(nc.psum_tensor("pf%s%d" % (tag, i), [128, 512], F32)))
            fb["ln_fm"] = E(nc.sbuf_tensor("ln_fm" + tag, [128, 8 * MC], BF16))
            fb["y_acc"] = E(nc.sbuf_tensor("y_acc" + tag, [128, NTM * 1024], F32))
            fb["w13b"] = [E(nc.sbuf_tensor("w13_%s%d" % (tag, i), [128, 8 * 1024], BF16)) for i in range(2)]
            fb["w2b"] = [E(nc.sbuf_tensor("w2_%s%d" % (tag, i), [128, 4 * 1024], BF16)) for i in range(2)]
            fb["xsl"] = [E(nc.sbuf_tensor("xf%s%d" % (tag, i), [128, 1024], F32)) for i in range(2)]
            fb["ln_tm"] = [E(nc.sbuf_tensor("lt%s%d" % (tag, i), [128, 1024], BF16)) for i in range(2)]
            fb["h_sb"] = E(nc.sbuf_tensor("h" + tag, [128, 4 * MC], BF16))
            fb["sg"] = [E(nc.sbuf_tensor("sg%s%d" % (tag, i), [128, 512], BF16)) for i in range(2)]
            fb["sc_tm"] = E(nc.sbuf_tensor("sc" + tag, [128, NTM * 16], F32))
            fb["bt"] = [E(nc.sbuf_tensor("bt%s%d" % (tag, i), [128, NTM * 16], F32)) for i in range(2)]
            fb["msk"] = E(nc.sbuf_tensor("mk" + tag, [128, NTM * 16], F32))
            fb["gsum"] = E(nc.sbuf_tensor("gs" + tag, [128, NTM], F32))
            fb["stt"] = E(nc.sbuf_tensor("stt" + tag, [128, 12], F32))
            fb["mv"] = E(nc.sbuf_tensor("mv" + tag, [128, 2 * NTM], F32))
            fb["rstd"] = E(nc.sbuf_tensor("rs" + tag, [128, NTM], F32))
            fb["xsl_free"] = [None, None]
            fb["ln_tm_free"] = [None, None]
            return fb

        with contextlib.ExitStack() as stack:
            fb = make_ffn_bufs(stack, "a")
            fb["gu_banks"] = [B[0], B[1], B[2]]
            fb["y_banks"] = [B[3], B[4], B[5]]
            bb = make_b1_bufs(stack.enter_context)
            bb["kvb"] = [B[6], B[7]]
            x1_done = [None] * (T // 128)
            bb["x1_done"] = x1_done

            def b1_hook(mc, e):
                if mc >= 1 and e == 5:
                    emit_b1_chunk(bb, (mc - 1) * 2)
                elif mc >= 1 and e == 11:
                    emit_b1_chunk(bb, (mc - 1) * 2 + 1)

            emit_ffn_pass(fb, x_in, x1, hook=b1_hook, dst_done=x1_done)
            emit_b1_chunk(bb, (NMC - 1) * 2)
            emit_b1_chunk(bb, (NMC - 1) * 2 + 1)
            quiesce(extra=[bst, blx[0], blx[1]])
            P.flush()
            nc.all_engine_barrier()

        with contextlib.ExitStack() as stack:
            E = stack.enter_context
            ab = {}
            ab["K_fm"] = E(nc.sbuf_tensor("K_fm", [128, 8 * T], BF16))
            ab["v_tm"] = E(nc.sbuf_tensor("v_tm", [128, 32 * 1040], BF16))
            ab["xsl"] = [E(nc.sbuf_tensor("xat%d" % i, [128, 1024], F32)) for i in range(2)]
            ab["ln_tm"] = [E(nc.sbuf_tensor("ltat%d" % i, [128, 1024], BF16)) for i in range(2)]
            ab["stt"] = E(nc.sbuf_tensor("sttat", [128, 12], F32))
            ab["mv"] = E(nc.sbuf_tensor("mvat", [128, 8], F32))
            ab["rstd"] = E(nc.sbuf_tensor("rsat", [128, 4], F32))
            ab["xsl_free"] = [None, None]
            ab["ln_tm_free"] = [None, None]
            for i in range(8):
                B[i] = Bank(E(nc.psum_tensor("pb2_%d" % i, [128, 512], F32)))
            ab["wq_sb"] = E(nc.sbuf_tensor("wq_sb", [128, 8 * 1024], BF16))
            ab["wo_sb"] = E(nc.sbuf_tensor("wo_sb", [128, 8 * 1024], BF16))
            ab["E2"] = [E(nc.sbuf_tensor("Eb%d" % i, [128, 16 * 128], BF16)) for i in range(4)]
            ab["q_sb"] = E(nc.sbuf_tensor("q_sb", [128, 8 * 128], BF16))
            ab["at_sb"] = E(nc.sbuf_tensor("at_sb", [128, 1024], BF16))
            ab["cc_sb"] = E(nc.sbuf_tensor("cc_sb", [128, 8 * 128], BF16))
            ab["ln_fmq"] = E(nc.sbuf_tensor("ln_fmq", [128, 8 * 128], BF16))
            ab["den"] = E(nc.sbuf_tensor("den", [128, 1], F32))
            for ic in range(8):
                P.op("gpsimd", lambda g, ic=ic: g.dma_start(
                    ab["K_fm"][:, ic * T:(ic + 1) * T], k_scr[ic]),
                    inc=ldi, incn=16)
            for hp in range(8):
                P.op("gpsimd", lambda g, hp=hp: g.dma_start(
                    AP(tensor=ab["v_tm"].ap().tensor, offset=hp * 130,
                       ap=[ab["v_tm"].ap().ap[0], [1040, 32], [1, 130]]),
                    v_scr[hp]), inc=ldi, incn=16)
            ab["kv_snap"] = (ldi, ldi.v)
            emit_attn_b2(ab)
            quiesce()
            P.flush()
            nc.all_engine_barrier()

        with contextlib.ExitStack() as stack:
            fb = make_ffn_bufs(stack, "b")
            emit_ffn_pass(fb, x2, out)
            P.op("sync", lambda sy: sy.nop(), waits=[st2[0], st2[1]])
            quiesce()
            P.flush()
            nc.all_engine_barrier()

    return nc


_NC_CACHE = {}


def kernel(L0_attn_hidden_states, L0_attn_Wq, L0_attn_Wkv, L0_attn_Wo, L0_attn_bo,
           L0_ffn_input_ids, L0_ffn_gate_w, L0_ffn_gate_b, L0_ffn_w13, L0_ffn_w2,
           L0_ffn_w13_s, L0_ffn_w2_s):
    x = np.ascontiguousarray(np.asarray(L0_attn_hidden_states, dtype=np.float32))
    Wq = np.ascontiguousarray(np.asarray(L0_attn_Wq, dtype=np.float32))
    Wkv = np.ascontiguousarray(np.asarray(L0_attn_Wkv, dtype=np.float32))
    Wo = np.ascontiguousarray(np.asarray(L0_attn_Wo, dtype=np.float32))
    bo = np.asarray(L0_attn_bo, dtype=np.float32)
    gate_w = np.asarray(L0_ffn_gate_w, dtype=np.float32)
    gate_b = np.asarray(L0_ffn_gate_b, dtype=np.float32)
    w13 = np.asarray(L0_ffn_w13, dtype=np.float32)
    w2 = np.asarray(L0_ffn_w2, dtype=np.float32)
    w13_s = np.asarray(L0_ffn_w13_s, dtype=np.float32)
    w2_s = np.asarray(L0_ffn_w2_s, dtype=np.float32)

    w13t_ = np.ascontiguousarray(
        np.concatenate([w13.transpose(0, 2, 1), w13_s.T[None]], axis=0))
    w2t_ = np.ascontiguousarray(
        np.concatenate([w2.transpose(0, 2, 1), w2_s.T[None]], axis=0))
    gwt_ = np.ascontiguousarray(gate_w.T)
    gb_b_ = np.ascontiguousarray(
        np.broadcast_to(np.tile(gate_b, NTM)[None, :], (128, NTM * 16)))
    bo_b_ = np.ascontiguousarray(np.broadcast_to(bo[None, :], (128, D)))

    if "nc" not in _NC_CACHE:
        _NC_CACHE["nc"] = build_program()
    nc = _NC_CACHE["nc"]

    in_maps = []
    for b in range(N_CORES):
        in_maps.append({
            "x_in": np.ascontiguousarray(x[b]),
            "w13t": w13t_, "w2t": w2t_, "gwt": gwt_, "gb_b": gb_b_,
            "wq": Wq, "wkv": Wkv, "wo": Wo, "bo_b": bo_b_,
        })
    res = run_bass_kernel_spmd(nc, in_maps, core_ids=list(range(N_CORES)))
    return np.stack([r["out"] for r in res.results], axis=0)
